# version 04 (final)
# baseline (speedup 1.0000x reference)
import numpy as np
import ml_dtypes
import contextlib
import concourse.bass as bass
import concourse.mybir as mybir
from concourse.bass_utils import run_bass_kernel_spmd

F32 = mybir.dt.float32
BF16 = mybir.dt.bfloat16
I32 = mybir.dt.int32
AF = mybir.ActivationFunctionType
ALU = mybir.AluOpType
AX = mybir.AxisListType
BF = ml_dtypes.bfloat16
NC = 8
D = 4096
T = 8192
RPC = 1024
S = 4096

SEMS = ("lds", "ldg", "mm", "ve", "se", "stg")


class Em:

    def __init__(self, sems, cur, eng):
        self.sems = sems
        self.cur = cur
        self.eng = eng
        self.cnt = {k: 0 for k in sems}

    def op(self, ename, fn, inc=None, k=1):
        if self.cur == ename and ename in ("vector", "scalar"):
            sem = "ve" if ename == "vector" else "se"
            self.eng.wait_ge(self.sems[sem], self.cnt[sem])
        if self.cur == ename:
            ins = fn(self.eng)
            if inc is not None:
                ins.then_inc(self.sems[inc], k)
        if inc is not None:
            self.cnt[inc] += k

    def dma(self, ename, out, in_, sem):
        self.op(ename, lambda e: e.dma_start(out=out, in_=in_), sem, 16)

    def wait(self, ename, sem, n=None):
        if self.cur == ename:
            self.eng.wait_ge(self.sems[sem], self.cnt[sem] if n is None else n)


def run_launch(body, tensors, sbufs, psums, in_maps):
    nc = bass.Bass("TRN2", target_bir_lowering=False)
    aps = {}
    for (name, shape, dt, kind) in tensors:
        aps[name] = nc.dram_tensor(name, list(shape), dt, kind=kind).ap()
    with contextlib.ExitStack() as st:
        sb = {}
        for (name, shape, dt) in sbufs:
            sb[name] = st.enter_context(nc.sbuf_tensor(name, list(shape), dt))
        for (name, shape, dt) in psums:
            sb[name] = st.enter_context(nc.psum_tensor(name, list(shape), dt))
        sems = {s: st.enter_context(nc.semaphore(s)) for s in SEMS}
        block = st.enter_context(nc.Block())
        for ename in ("sync", "gpsimd", "tensor", "vector", "scalar"):
            def f(eng, ename=ename):
                em = Em(sems, ename, eng)
                body(em, aps, sb)
                em.wait("gpsimd", "stg")
            getattr(block, ename)(f)
    res = run_bass_kernel_spmd(nc, in_maps, core_ids=list(range(NC)))
    return res.results


def tile3(ap, p=128):
    return ap.rearrange("(j p) t -> p j t", p=p)


def emit_transpose_seq(em, srcs, dsts, pt0, pt1, idb):
    n = len(srcs)
    G = 8
    gi = 0
    for s0 in range(0, n, G):
        grp = range(s0, min(s0 + G, n))
        pt = pt0 if gi % 2 == 0 else pt1
        em.wait("tensor", "ve")
        for k, idx in enumerate(grp):
            em.op("tensor", lambda t, k=k, idx=idx, pt=pt: t.transpose(
                pt[:, k * 128:(k + 1) * 128], srcs[idx], idb[:, :]), "mm")
        em.wait("vector", "mm")
        for k, idx in enumerate(grp):
            em.op("vector", lambda v, k=k, idx=idx, pt=pt: v.tensor_copy(
                out=dsts[idx], in_=pt[:, k * 128:(k + 1) * 128]), "ve")
        gi += 1


def emit_bcast(em, ones, row, dst, W, ps):
    for i in range(0, W, 512):
        n = min(512, W - i)
        em.wait("tensor", "ve")
        em.op("tensor", lambda t, i=i, n=n: t.matmul(
            ps[:, 0:n], ones[:, :], row[:, i:i + n], start=True, stop=True), "mm")
        em.wait("vector", "mm")
        em.op("vector", lambda v, i=i, n=n: v.tensor_copy(out=dst[:, i:i + n], in_=ps[:, 0:n]), "ve")


def emit_rstd(em, y_ap, W, eps, stt, sq_ap):
    em.op("scalar", lambda s: s.activation(out=sq_ap, in_=y_ap, func=AF.Square), "se")
    em.wait("vector", "se")
    em.op("vector", lambda v: v.reduce_sum(stt[:, 0:1], sq_ap, axis=AX.X), "ve")
    em.op("vector", lambda v: v.tensor_scalar(out=stt[:, 1:2], in0=stt[:, 0:1],
          scalar1=1.0 / W, scalar2=None, op0=ALU.mult), "ve")
    em.op("vector", lambda v: v.tensor_scalar(out=stt[:, 1:2], in0=stt[:, 1:2],
          scalar1=eps, scalar2=None, op0=ALU.add), "ve")
    em.wait("scalar", "ve")
    em.op("scalar", lambda s: s.sqrt(stt[:, 2:3], stt[:, 1:2]), "se")
    em.wait("vector", "se")
    em.op("vector", lambda v: v.reciprocal(stt[:, 3:4], stt[:, 2:3]), "ve")


def emit_ln(em, y, W, stt, sq, out_emit):
    em.op("vector", lambda v: v.reduce_sum(stt[:, 0:1], y, axis=AX.X), "ve")
    em.op("vector", lambda v: v.tensor_scalar(out=stt[:, 1:2], in0=stt[:, 0:1],
          scalar1=1.0 / W, scalar2=None, op0=ALU.mult), "ve")
    em.op("vector", lambda v: v.tensor_mul(sq, y, y), "ve")
    em.op("vector", lambda v: v.reduce_sum(stt[:, 2:3], sq, axis=AX.X), "ve")
    em.op("vector", lambda v: v.tensor_mul(stt[:, 3:4], stt[:, 1:2], stt[:, 1:2]), "ve")
    em.op("vector", lambda v: v.tensor_scalar(out=stt[:, 4:5], in0=stt[:, 2:3],
          scalar1=1.0 / W, scalar2=None, op0=ALU.mult), "ve")
    em.op("vector", lambda v: v.tensor_sub(stt[:, 5:6], stt[:, 4:5], stt[:, 3:4]), "ve")
    em.op("vector", lambda v: v.tensor_scalar(out=stt[:, 5:6], in0=stt[:, 5:6],
          scalar1=1e-5, scalar2=None, op0=ALU.add), "ve")
    em.wait("scalar", "ve")
    em.op("scalar", lambda s: s.sqrt(stt[:, 6:7], stt[:, 5:6]), "se")
    em.wait("vector", "se")
    em.op("vector", lambda v: v.reciprocal(stt[:, 7:8], stt[:, 6:7]), "ve")
    em.op("vector", lambda v: v.tensor_scalar(out=stt[:, 8:9], in0=stt[:, 1:2],
          scalar1=stt[:, 7:8], scalar2=None, op0=ALU.mult), "ve")
    em.op("vector", lambda v: v.tensor_scalar(out=stt[:, 8:9], in0=stt[:, 8:9],
          scalar1=-1.0, scalar2=None, op0=ALU.mult), "ve")
    em.wait("scalar", "ve")
    em.op("scalar", lambda s: s.activation(out=sq, in_=y, func=AF.Copy, scale=stt[:, 7:8]), "se")
    em.wait("vector", "se")
    em.op("vector", lambda v: v.tensor_scalar(out=sq, in0=sq, scalar1=stt[:, 8:9],
          scalar2=None, op0=ALU.add), "ve")
    out_emit()



def launch_l0_down(X, Wqa, lnq, Wkva_p, lnkv, cos0, sin0, ident):
    KW = 1536
    tensors = [
        ("x", (RPC, D), F32, "ExternalInput"),
        ("wqa", (D, KW), F32, "ExternalInput"),
        ("lnq", (1, KW), F32, "ExternalInput"),
        ("wkva", (D, 576), F32, "ExternalInput"),
        ("lnkv", (1, 512), F32, "ExternalInput"),
        ("cos", (RPC, 64), F32, "ExternalInput"),
        ("sin", (RPC, 64), F32, "ExternalInput"),
        ("ident", (128, 128), F32, "ExternalInput"),
        ("qanT", (KW, RPC), BF16, "ExternalOutput"),
        ("cnT", (512, RPC), BF16, "ExternalOutput"),
        ("kpeT", (64, RPC), BF16, "ExternalOutput"),
    ]
    sbufs = [
        ("wq", (128, 32 * KW), BF16),
        ("wkv", (128, 32 * 576), BF16),
        ("lnqr", (128, KW), F32),
        ("lnkvr", (128, 512), F32),
        ("rowq", (1, KW), F32),
        ("rowkv", (1, 512), F32),
        ("ones", (1, 128), F32),
        ("idb", (128, 128), BF16),
        ("xbf", (128, D), BF16),
        ("xT", (128, 32 * 128), BF16),
        ("tmp", (128, KW), F32),
        ("stt", (128, 16), F32),
        ("qanb", (128, KW), BF16),
        ("cnb", (128, 512), BF16),
        ("cost", (128, 64), F32),
        ("sint", (128, 64), F32),
        ("pe", (128, 64), F32),
        ("rot", (128, 64), F32),
        ("t1", (128, 64), F32),
        ("kpb", (128, 64), BF16),
        ("tstg", (128, 17 * 128), BF16),
    ]
    psums = [
        ("pmm", (128, 1536), F32),
        ("pkv", (128, 1024), F32),
        ("pt0", (128, 1024), BF16),
        ("pt1", (128, 1024), BF16),
    ]

    def body(em, aps, sb):
        for j in range(32):
            em.dma("gpsimd", sb["wq"][:, j * KW:(j + 1) * KW], aps["wqa"][j * 128:(j + 1) * 128, :], "ldg")
            em.dma("gpsimd", sb["wkv"][:, j * 576:(j + 1) * 576], aps["wkva"][j * 128:(j + 1) * 128, :], "ldg")
        em.dma("gpsimd", sb["idb"][:, :], aps["ident"][:, :], "ldg")
        em.dma("sync", sb["rowq"][:, :], aps["lnq"][:, :], "lds")
        em.dma("sync", sb["rowkv"][:, :], aps["lnkv"][:, :], "lds")
        em.op("vector", lambda v: v.memset(sb["ones"][:, :], 1.0), "ve")
        em.wait("tensor", "lds")
        em.wait("tensor", "ve")
        emit_bcast(em, sb["ones"], sb["rowq"], sb["lnqr"], KW, sb["pmm"])
        emit_bcast(em, sb["ones"], sb["rowkv"], sb["lnkvr"], 512, sb["pkv"])
        for t in range(8):
            em.wait("gpsimd", "mm")
            em.dma("gpsimd", sb["xbf"][:, :], aps["x"][t * 128:(t + 1) * 128, :], "ldg")
            em.wait("sync", "ve")
            em.dma("sync", sb["cost"][:, :], aps["cos"][t * 128:(t + 1) * 128, :], "lds")
            em.dma("sync", sb["sint"][:, :], aps["sin"][t * 128:(t + 1) * 128, :], "lds")
            em.wait("tensor", "ldg")
            em.wait("tensor", "se")
            emit_transpose_seq(em, [sb["xbf"][:, j * 128:(j + 1) * 128] for j in range(32)],
                               [sb["xT"][:, j * 128:(j + 1) * 128] for j in range(32)],
                               sb["pt0"], sb["pt1"], sb["idb"])
            em.wait("tensor", "ve")
            for ncol in range(3):
                for j in range(32):
                    em.op("tensor", lambda tt, ncol=ncol, j=j: tt.matmul(
                        sb["pmm"][:, ncol * 512:(ncol + 1) * 512],
                        sb["xT"][:, j * 128:(j + 1) * 128],
                        sb["wq"][:, j * KW + ncol * 512: j * KW + (ncol + 1) * 512],
                        start=(j == 0), stop=(j == 31)), "mm")
            for (n0, n1) in ((0, 512), (512, 576)):
                for j in range(32):
                    em.op("tensor", lambda tt, n0=n0, n1=n1, j=j: tt.matmul(
                        sb["pkv"][:, n0:n1],
                        sb["xT"][:, j * 128:(j + 1) * 128],
                        sb["wkv"][:, j * 576 + n0: j * 576 + n1],
                        start=(j == 0), stop=(j == 31)), "mm")
            em.wait("vector", "mm")
            em.wait("scalar", "mm")
            em.wait("vector", "stg")
            emit_rstd(em, sb["pmm"][:, 0:KW], KW, 1e-6, sb["stt"], sb["tmp"][:, :])
            em.wait("scalar", "ve")
            em.op("scalar", lambda s: s.activation(out=sb["tmp"][:, :], in_=sb["pmm"][:, 0:KW],
                  func=AF.Copy, scale=sb["stt"][:, 3:4]), "se")
            em.wait("vector", "se")
            em.op("vector", lambda v: v.tensor_mul(sb["qanb"][:, :], sb["tmp"][:, :], sb["lnqr"][:, :]), "ve")
            em.wait("scalar", "ve")
            emit_rstd(em, sb["pkv"][:, 0:512], 512, 1e-6, sb["stt"], sb["tmp"][:, 0:512])
            em.wait("scalar", "ve")
            em.op("scalar", lambda s: s.activation(out=sb["tmp"][:, 0:512], in_=sb["pkv"][:, 0:512],
                  func=AF.Copy, scale=sb["stt"][:, 3:4]), "se")
            em.wait("vector", "se")
            em.op("vector", lambda v: v.tensor_mul(sb["cnb"][:, :], sb["tmp"][:, 0:512], sb["lnkvr"][:, :]), "ve")
            em.wait("vector", "lds")
            em.op("vector", lambda v: v.tensor_copy(out=sb["pe"][:, :], in_=sb["pkv"][:, 512:576]), "ve")
            em.op("vector", lambda v: v.tensor_mul(sb["t1"][:, :], sb["pe"][:, :], sb["cost"][:, :]), "ve")
            em.op("vector", lambda v: v.tensor_scalar(out=sb["rot"][:, 0:32], in0=sb["pe"][:, 32:64],
                  scalar1=-1.0, scalar2=None, op0=ALU.mult), "ve")
            em.op("vector", lambda v: v.tensor_copy(out=sb["rot"][:, 32:64], in_=sb["pe"][:, 0:32]), "ve")
            em.op("vector", lambda v: v.tensor_mul(sb["rot"][:, :], sb["rot"][:, :], sb["sint"][:, :]), "ve")
            em.op("vector", lambda v: v.tensor_add(sb["kpb"][:, :], sb["t1"][:, :], sb["rot"][:, :]), "ve")
            srcs = [sb["qanb"][:, j * 128:(j + 1) * 128] for j in range(12)] + \
                   [sb["cnb"][:, j * 128:(j + 1) * 128] for j in range(4)]
            dsts = [sb["tstg"][:, j * 128:(j + 1) * 128] for j in range(16)]
            emit_transpose_seq(em, srcs, dsts, sb["pt0"], sb["pt1"], sb["idb"])
            em.wait("tensor", "ve")
            em.op("tensor", lambda tt: tt.transpose(sb["pt0"][0:64, 0:128], sb["kpb"][:, :], sb["idb"][:, :]), "mm")
            em.wait("vector", "mm")
            em.op("vector", lambda v: v.tensor_copy(out=sb["tstg"][0:64, 16 * 128:17 * 128],
                  in_=sb["pt0"][0:64, 0:128]), "ve")
            em.wait("gpsimd", "ve")
            for j in range(12):
                em.dma("gpsimd", aps["qanT"][j * 128:(j + 1) * 128, t * 128:(t + 1) * 128],
                       sb["tstg"][:, j * 128:(j + 1) * 128], "stg")
            for j in range(4):
                em.dma("gpsimd", aps["cnT"][j * 128:(j + 1) * 128, t * 128:(t + 1) * 128],
                       sb["tstg"][:, (12 + j) * 128:(13 + j) * 128], "stg")
            em.dma("gpsimd", aps["kpeT"][0:64, t * 128:(t + 1) * 128],
                   sb["tstg"][0:64, 16 * 128:17 * 128], "stg")

    in_maps = []
    for c in range(NC):
        p0 = (c % 4) * 1024
        in_maps.append({
            "x": X[c * RPC:(c + 1) * RPC], "wqa": Wqa, "lnq": lnq.reshape(1, -1),
            "wkva": Wkva_p, "lnkv": lnkv.reshape(1, -1),
            "cos": cos0[p0:p0 + RPC], "sin": sin0[p0:p0 + RPC], "ident": ident,
        })
    res = run_launch(body, tensors, sbufs, psums, in_maps)
    return ([r["qanT"] for r in res], [r["cnT"] for r in res], [r["kpeT"] for r in res])


def launch_l0_qb(qanT_l, Wqb_p, cos0, sin0, ident):
    KW = 1536
    SCL = float(192.0 ** -0.5)
    tensors = [
        ("qanT", (KW, RPC), BF16, "ExternalInput"),
        ("wqb", (KW, 6144), F32, "ExternalInput"),
        ("cos", (RPC, 64), F32, "ExternalInput"),
        ("sin", (RPC, 64), F32, "ExternalInput"),
        ("ident", (128, 128), F32, "ExternalInput"),
        ("qT", (6144, RPC), BF16, "ExternalOutput"),
    ]
    sbufs = [
        ("wq", (128, 12 * 3072), BF16),
        ("idb", (128, 128), BF16),
        ("qat", (128, 12 * 128), BF16),
        ("q2b", (128, 3072), BF16),
        ("pef", (128, 2048), F32),
        ("t1", (128, 2048), F32),
        ("rot", (128, 2048), F32),
        ("cost", (128, 64), F32),
        ("sint", (128, 64), F32),
        ("tstg", (128, 24 * 128), BF16),
    ]
    psums = [
        ("pmm", (128, 3072), F32),
        ("pt0", (128, 1024), BF16),
        ("pt1", (128, 1024), BF16),
    ]

    def body(em, aps, sb):
        em.dma("gpsimd", sb["idb"][:, :], aps["ident"][:, :], "ldg")
        for ph in range(2):
            c0 = ph * 3072
            for j in range(12):
                em.wait("gpsimd", "mm")
                em.dma("gpsimd", sb["wq"][:, j * 3072:(j + 1) * 3072],
                       aps["wqb"][j * 128:(j + 1) * 128, c0:c0 + 3072], "ldg")
            for t in range(8):
                em.wait("sync", "mm")
                em.wait("sync", "ve")
                em.dma("sync", sb["qat"][:, :].rearrange("p (j t) -> p j t", t=128),
                       tile3(aps["qanT"])[:, :, t * 128:(t + 1) * 128], "lds")
                em.dma("sync", sb["cost"][:, :], aps["cos"][t * 128:(t + 1) * 128, :], "lds")
                em.dma("sync", sb["sint"][:, :], aps["sin"][t * 128:(t + 1) * 128, :], "lds")
                em.wait("tensor", "lds")
                em.wait("tensor", "ldg")
                em.wait("tensor", "se")
                em.wait("tensor", "ve")
                for ncol in range(6):
                    for j in range(12):
                        em.op("tensor", lambda tt, ncol=ncol, j=j: tt.matmul(
                            sb["pmm"][:, ncol * 512:(ncol + 1) * 512],
                            sb["qat"][:, j * 128:(j + 1) * 128],
                            sb["wq"][:, j * 3072 + ncol * 512: j * 3072 + (ncol + 1) * 512],
                            start=(j == 0), stop=(j == 11)), "mm")
                em.wait("scalar", "mm")
                em.wait("vector", "mm")
                if ph == 0:
                    em.op("scalar", lambda s: s.activation(out=sb["q2b"][:, :], in_=sb["pmm"][:, :],
                          func=AF.Copy, scale=SCL), "se")
                else:
                    em.op("scalar", lambda s: s.activation(out=sb["q2b"][:, 0:1024], in_=sb["pmm"][:, 0:1024],
                          func=AF.Copy, scale=SCL), "se")
                    em.op("vector", lambda v: v.tensor_copy(out=sb["pef"][:, :], in_=sb["pmm"][:, 1024:3072]), "ve")
                    pe3 = sb["pef"][:, :].rearrange("p (h f) -> p h f", f=64)
                    t13 = sb["t1"][:, :].rearrange("p (h f) -> p h f", f=64)
                    ro3 = sb["rot"][:, :].rearrange("p (h f) -> p h f", f=64)
                    cosb = sb["cost"][:, :].rearrange("p (o f) -> p o f", o=1)
                    sinb = sb["sint"][:, :].rearrange("p (o f) -> p o f", o=1)

                    def tt_b(v, outap, a, b):
                        a2, b2 = bass.broadcast_tensor_aps(a, b)
                        return v.tensor_tensor(out=outap, in0=a2, in1=b2, op=ALU.mult)
                    em.op("vector", lambda v: tt_b(v, t13, pe3, cosb), "ve")
                    em.op("vector", lambda v: v.tensor_scalar(out=ro3[:, :, 0:32], in0=pe3[:, :, 32:64],
                          scalar1=-1.0, scalar2=None, op0=ALU.mult), "ve")
                    em.op("vector", lambda v: v.tensor_copy(out=ro3[:, :, 32:64], in_=pe3[:, :, 0:32]), "ve")
                    em.op("vector", lambda v: tt_b(v, ro3, ro3, sinb), "ve")
                    em.op("vector", lambda v: v.tensor_add(sb["t1"][:, :], sb["t1"][:, :], sb["rot"][:, :]), "ve")
                    em.wait("scalar", "ve")
                    em.op("scalar", lambda s: s.activation(out=sb["q2b"][:, 1024:3072], in_=sb["t1"][:, :],
                          func=AF.Copy, scale=SCL), "se")
                em.wait("tensor", "se")
                em.wait("vector", "stg")
                emit_transpose_seq(em, [sb["q2b"][:, j * 128:(j + 1) * 128] for j in range(24)],
                                   [sb["tstg"][:, j * 128:(j + 1) * 128] for j in range(24)],
                                   sb["pt0"], sb["pt1"], sb["idb"])
                em.wait("gpsimd", "ve")
                for j in range(24):
                    em.dma("gpsimd", aps["qT"][c0 + j * 128:c0 + (j + 1) * 128, t * 128:(t + 1) * 128],
                           sb["tstg"][:, j * 128:(j + 1) * 128], "stg")

    in_maps = []
    for c in range(NC):
        p0 = (c % 4) * 1024
        in_maps.append({
            "qanT": qanT_l[c], "wqb": Wqb_p,
            "cos": cos0[p0:p0 + RPC], "sin": sin0[p0:p0 + RPC], "ident": ident,
        })
    res = run_launch(body, tensors, sbufs, psums, in_maps)
    return [r["qT"] for r in res]


def launch_l0_kvup(cnT_l, Wkvb_p):
    tensors = [
        ("cnT", (512, RPC), BF16, "ExternalInput"),
        ("wkvb", (512, 8192), F32, "ExternalInput"),
        ("kT", (4096, RPC), BF16, "ExternalOutput"),
        ("v", (RPC, 4096), BF16, "ExternalOutput"),
    ]
    sbufs = [
        ("wb", (128, 4 * 8192), BF16),
        ("cn", (128, 4 * 1024), BF16),
        ("es", (128, 1024), BF16),
    ]
    psums = [
        ("pk0", (128, 512), F32),
        ("pk1", (128, 512), F32),
    ]

    def body(em, aps, sb):
        for j in range(4):
            em.dma("gpsimd", sb["wb"][:, j * 8192:(j + 1) * 8192], aps["wkvb"][j * 128:(j + 1) * 128, :], "ldg")
            em.dma("sync", sb["cn"][:, j * 1024:(j + 1) * 1024], aps["cnT"][j * 128:(j + 1) * 128, :], "lds")
        em.wait("tensor", "ldg")
        em.wait("tensor", "lds")
        bi = 0

        def unit(mm_emit, outap):
            nonlocal bi
            ps = sb["pk0"] if bi % 2 == 0 else sb["pk1"]
            eslot = sb["es"][:, (bi % 2) * 512:(bi % 2) * 512 + 512]
            em.wait("tensor", "ve")
            em.wait("tensor", "stg" if False else "ve")
            mm_emit(ps)
            em.wait("vector", "mm")
            em.wait("vector", "stg")
            em.op("vector", lambda v, ps=ps, eslot=eslot: v.tensor_copy(out=eslot, in_=ps[:, :]), "ve")
            em.wait("gpsimd", "ve")
            em.dma("gpsimd", outap, eslot, "stg")
            bi += 1

        for h in range(32):
            for kc in range(2):
                def mk(ps, h=h, kc=kc):
                    for j in range(4):
                        em.op("tensor", lambda tt, j=j, h=h, kc=kc, ps=ps: tt.matmul(
                            ps[:, :], sb["wb"][:, j * 8192 + h * 128: j * 8192 + (h + 1) * 128],
                            sb["cn"][:, j * 1024 + kc * 512: j * 1024 + (kc + 1) * 512],
                            start=(j == 0), stop=(j == 3)), "mm")
                unit(mk, aps["kT"][h * 128:(h + 1) * 128, kc * 512:(kc + 1) * 512])
        for g in range(8):
            for kt in range(8):
                def mk(ps, g=g, kt=kt):
                    for j in range(4):
                        em.op("tensor", lambda tt, j=j, g=g, kt=kt, ps=ps: tt.matmul(
                            ps[:, :], sb["cn"][:, j * 1024 + kt * 128: j * 1024 + (kt + 1) * 128],
                            sb["wb"][:, j * 8192 + 4096 + g * 512: j * 8192 + 4096 + (g + 1) * 512],
                            start=(j == 0), stop=(j == 3)), "mm")
                unit(mk, aps["v"][kt * 128:(kt + 1) * 128, g * 512:(g + 1) * 512])

    in_maps = [{"cnT": cnT_l[c], "wkvb": Wkvb_p} for c in range(NC)]
    res = run_launch(body, tensors, sbufs, psums, in_maps)
    return [r["kT"] for r in res], [r["v"] for r in res]


def launch_attn(mode, qTn_l, kT_l, v_l, qTp_l=None, kpeT_l=None, tri=None, slopes_l=None):
    is_l0 = (mode == "l0")
    tensors = [
        ("qTn", (1024, S), BF16, "ExternalInput"),
        ("kT8", (1024, S), BF16, "ExternalInput"),
        ("v8", (S, 1024), BF16, "ExternalInput"),
        ("ident", (128, 128), F32, "ExternalInput"),
        ("tri", (128, 128), F32, "ExternalInput"),
        ("OT8", (1024, S), BF16, "ExternalOutput"),
    ]
    if is_l0:
        tensors += [
            ("qTp", (512, S), BF16, "ExternalInput"),
            ("kpeT", (64, S), BF16, "ExternalInput"),
        ]
    else:
        tensors += [
            ("slopes", (1, 8), F32, "ExternalInput"),
            ("jrow", (1, S), F32, "ExternalInput"),
        ]
    sbufs = [
        ("qh", (128, S), BF16),
        ("kh", (128, S), BF16),
        ("vh", (128, 32 * 128), BF16),
        ("ssb", (128, 2 * S), F32),
        ("pb", (128, S), BF16),
        ("pts", (128, 32 * 128), BF16),
        ("obf", (128, 128), BF16),
        ("stt", (128, 16), F32),
        ("idb", (128, 128), BF16),
        ("trib", (128, 128), F32),
    ]
    if is_l0:
        sbufs += [("qph", (64, S), BF16), ("kpe", (64, S), BF16)]
    else:
        sbufs += [("plane", (128, S), F32), ("prow", (1, S), F32), ("prow2", (1, S), F32),
                  ("srow", (1, 8), F32), ("ones", (1, 128), F32)]
    psums = [
        ("ps0", (128, 512), F32),
        ("ps1", (128, 512), F32),
        ("pt0", (128, 1024), BF16),
        ("pt1", (128, 1024), BF16),
        ("po", (128, 128), F32),
        ("pbc", (128, 512), F32),
    ]

    def body(em, aps, sb):
        em.dma("gpsimd", sb["idb"][:, :], aps["ident"][:, :], "ldg")
        em.dma("sync", sb["trib"][:, :], aps["tri"][:, :], "lds")
        if is_l0:
            em.dma("sync", sb["kpe"][0:64, :], aps["kpeT"][:, :], "lds")
        else:
            em.dma("sync", sb["srow"][:, :], aps["slopes"][:, :], "lds")
            em.dma("sync", sb["prow"][:, :], aps["jrow"][:, :], "lds")
            em.op("vector", lambda v: v.memset(sb["ones"][:, :], 1.0), "ve")
        state = {"tmark": 0}
        for hh in range(8):
            em.wait("sync", "mm")
            em.dma("sync", sb["qh"][:, :], aps["qTn"][hh * 128:(hh + 1) * 128, :], "lds")
            em.dma("sync", sb["kh"][:, :], aps["kT8"][hh * 128:(hh + 1) * 128, :], "lds")
            em.dma("sync", sb["vh"][:, :].rearrange("p (j t) -> p j t", t=128),
                   tile3(aps["v8"])[:, :, hh * 128:(hh + 1) * 128], "lds")
            if is_l0:
                em.dma("sync", sb["qph"][0:64, :], aps["qTp"][hh * 64:(hh + 1) * 64, :], "lds")
            em.wait("tensor", "lds")
            if not is_l0:
                em.wait("vector", "lds")
                em.wait("vector", "mm")
                em.op("vector", lambda v, hh=hh: v.tensor_scalar(out=sb["prow2"][:, :], in0=sb["prow"][:, :],
                      scalar1=sb["srow"][:, hh:hh + 1], scalar2=None, op0=ALU.mult), "ve")
                em.wait("tensor", "ve")
                for i in range(0, S, 512):
                    if i >= 1024:
                        em.wait("tensor", "ve")
                    em.op("tensor", lambda t, i=i: t.matmul(
                        sb["pbc"][:, 0:512] if (i // 512) % 2 == 0 else sb["ps0"][:, 0:512],
                        sb["ones"][:, :], sb["prow2"][:, i:i + 512],
                        start=True, stop=True), "mm")
                    em.wait("vector", "mm")
                    em.op("vector", lambda v, i=i: v.tensor_copy(
                        out=sb["plane"][:, i:i + 512],
                        in_=sb["pbc"][:, 0:512] if (i // 512) % 2 == 0 else sb["ps0"][:, 0:512]), "ve")
            marks = {}

            def s1(qt):
                nk = 128 * (qt + 1)
                nchunk = (nk + 511) // 512
                par = qt % 2
                sv = sb["ssb"][:, par * S: par * S + nk]
                csem = "se" if is_l0 else "ve"
                em.wait("tensor", csem)
                if not is_l0:
                    em.wait("vector", "se")
                for kc in range(nchunk):
                    n0 = kc * 512
                    n1 = min(nk, n0 + 512)
                    ps = sb["ps0"] if kc % 2 == 0 else sb["ps1"]
                    if kc >= 2:
                        em.wait("tensor", csem)
                    em.op("tensor", lambda t, ps=ps, n0=n0, n1=n1, qt=qt: t.matmul(
                        ps[:, 0:n1 - n0], sb["qh"][:, qt * 128:(qt + 1) * 128], sb["kh"][:, n0:n1],
                        start=True, stop=(not is_l0)), "mm")
                    if is_l0:
                        em.op("tensor", lambda t, ps=ps, n0=n0, n1=n1, qt=qt: t.matmul(
                            ps[:, 0:n1 - n0], sb["qph"][0:64, qt * 128:(qt + 1) * 128], sb["kpe"][0:64, n0:n1],
                            start=False, stop=True), "mm")
                    if is_l0:
                        em.wait("scalar", "mm")
                        em.op("scalar", lambda s, ps=ps, n0=n0, n1=n1, par=par: s.activation(
                            out=sb["ssb"][:, par * S + n0: par * S + n1], in_=ps[:, 0:n1 - n0],
                            func=AF.Copy), "se")
                    else:
                        em.wait("vector", "mm")
                        em.op("vector", lambda v, ps=ps, n0=n0, n1=n1, par=par: v.tensor_add(
                            sb["ssb"][:, par * S + n0: par * S + n1], sb["plane"][:, n0:n1],
                            ps[:, 0:n1 - n0]), "ve")
                marks[qt] = em.cnt["se" if is_l0 else "ve"]

            def s2(qt):
                nk = 128 * (qt + 1)
                ntile = qt + 1
                par = qt % 2
                sv = sb["ssb"][:, par * S: par * S + nk]
                sttv = sb["stt"][:, 8 * par: 8 * par + 8]
                if is_l0:
                    em.wait("vector", "se", marks[qt])
                em.op("vector", lambda v, nk=nk, sv=sv: v.tensor_add(sv[:, nk - 128:nk],
                      sv[:, nk - 128:nk], sb["trib"][:, :]), "ve")
                em.op("vector", lambda v, sv=sv: v.reduce_max(sttv[:, 0:1], sv, axis=AX.X), "ve")
                em.op("vector", lambda v: v.tensor_scalar(out=sttv[:, 1:2], in0=sttv[:, 0:1],
                      scalar1=-1.0, scalar2=None, op0=ALU.mult), "ve")
                em.wait("scalar", "ve")
                em.wait("scalar", "mm", state["tmark"])
                em.op("scalar", lambda s, nk=nk, sv=sv: s.activation(out=sb["pb"][:, 0:nk], in_=sv,
                      func=AF.Exp, bias=sttv[:, 1:2]), "se")
                em.wait("vector", "se")
                em.op("vector", lambda v, nk=nk: v.reduce_sum(sttv[:, 2:3], sb["pb"][:, 0:nk], axis=AX.X), "ve")
                em.op("vector", lambda v: v.reciprocal(sttv[:, 3:4], sttv[:, 2:3]), "ve")
                em.wait("scalar", "ve")
                em.op("scalar", lambda s, nk=nk: s.activation(out=sb["pb"][:, 0:nk], in_=sb["pb"][:, 0:nk],
                      func=AF.Copy, scale=sttv[:, 3:4]), "se")
                em.wait("tensor", "se")
                emit_transpose_seq(em, [sb["pb"][:, k * 128:(k + 1) * 128] for k in range(ntile)],
                                   [sb["pts"][:, k * 128:(k + 1) * 128] for k in range(ntile)],
                                   sb["pt0"], sb["pt1"], sb["idb"])
                em.wait("tensor", "ve")
                for kt in range(ntile):
                    em.op("tensor", lambda t, kt=kt, ntile=ntile: t.matmul(
                        sb["po"][:, :], sb["vh"][:, kt * 128:(kt + 1) * 128],
                        sb["pts"][:, kt * 128:(kt + 1) * 128],
                        start=(kt == 0), stop=(kt == ntile - 1)), "mm")
                state["tmark"] = em.cnt["mm"]
                em.wait("vector", "mm")
                em.wait("vector", "stg")
                em.op("vector", lambda v: v.tensor_copy(out=sb["obf"][:, :], in_=sb["po"][:, :]), "ve")
                em.wait("gpsimd", "ve")
                em.dma("gpsimd", aps["OT8"][hh * 128:(hh + 1) * 128, qt * 128:(qt + 1) * 128],
                       sb["obf"][:, :], "stg")

            s1(0)
            for qt in range(32):
                if qt + 1 < 32:
                    s1(qt + 1)
                s2(qt)

    in_maps = []
    for c in range(NC):
        m = {"qTn": qTn_l[c], "kT8": kT_l[c], "v8": v_l[c], "ident": np.eye(128, dtype=np.float32),
             "tri": tri}
        if is_l0:
            m["qTp"] = qTp_l[c]
            m["kpeT"] = kpeT_l[c]
        else:
            m["slopes"] = slopes_l[c]
            m["jrow"] = np.arange(S, dtype=np.float32).reshape(1, S)
        in_maps.append(m)
    res = run_launch(body, tensors, sbufs, psums, in_maps)
    return [r["OT8"] for r in res]


def launch_oproj(OT_l, W, bias, xres_l, ident):
    has_b = bias is not None
    tensors = [
        ("OT", (4096, RPC), BF16, "ExternalInput"),
        ("wo", (4096, 4096), F32, "ExternalInput"),
        ("xres", (RPC, D), F32, "ExternalInput"),
        ("ident", (128, 128), F32, "ExternalInput"),
        ("xo", (RPC, D), F32, "ExternalOutput"),
        ("xoT", (4096, RPC), BF16, "ExternalOutput"),
        ("oq", (RPC, D), F32, "Internal"),
    ]
    if has_b:
        tensors.append(("bias", (1, 4096), F32, "ExternalInput"))
    sbufs = [
        ("wq", (128, 32 * 1024), BF16),
        ("idb", (128, 128), BF16),
        ("ot", (128, 32 * 128), BF16),
        ("es", (128, 1024), F32),
        ("y", (128, D), F32),
        ("sq", (128, D), F32),
        ("stt", (128, 16), F32),
        ("xb", (128, D), BF16),
        ("tstg", (128, 32 * 128), BF16),
        ("ones", (1, 128), F32),
        ("brow", (1, 4096), F32),
        ("brep", (128, 4096), F32),
    ]
    psums = [
        ("pq", (128, 1024), F32),
        ("pt0", (128, 1024), BF16),
        ("pt1", (128, 1024), BF16),
    ]

    def body(em, aps, sb):
        em.dma("gpsimd", sb["idb"][:, :], aps["ident"][:, :], "ldg")
        if has_b:
            em.dma("sync", sb["brow"][:, :], aps["bias"][:, :], "lds")
            em.op("vector", lambda v: v.memset(sb["ones"][:, :], 1.0), "ve")
            em.wait("tensor", "lds")
            em.wait("tensor", "ve")
            emit_bcast(em, sb["ones"], sb["brow"], sb["brep"], 4096, sb["pq"])
        for q in range(4):
            for j in range(32):
                em.wait("gpsimd", "mm")
                em.dma("gpsimd", sb["wq"][:, j * 1024:(j + 1) * 1024],
                       aps["wo"][j * 128:(j + 1) * 128, q * 1024:(q + 1) * 1024], "ldg")
            for t in range(8):
                em.wait("sync", "mm")
                em.dma("sync", sb["ot"][:, :].rearrange("p (j t) -> p j t", t=128),
                       tile3(aps["OT"])[:, :, t * 128:(t + 1) * 128], "lds")
                em.wait("tensor", "lds")
                em.wait("tensor", "ldg")
                em.wait("tensor", "ve")
                for ncol in range(2):
                    for j in range(32):
                        em.op("tensor", lambda tt, ncol=ncol, j=j: tt.matmul(
                            sb["pq"][:, ncol * 512:(ncol + 1) * 512],
                            sb["ot"][:, j * 128:(j + 1) * 128],
                            sb["wq"][:, j * 1024 + ncol * 512: j * 1024 + (ncol + 1) * 512],
                            start=(j == 0), stop=(j == 31)), "mm")
                em.wait("vector", "mm")
                em.wait("vector", "stg")
                em.op("vector", lambda v: v.tensor_copy(out=sb["es"][:, :], in_=sb["pq"][:, :]), "ve")
                em.wait("gpsimd", "ve")
                em.dma("gpsimd", aps["oq"][t * 128:(t + 1) * 128, q * 1024:(q + 1) * 1024], sb["es"][:, :], "stg")
        for t in range(8):
            em.wait("sync", "se")
            em.wait("sync", "stg")
            em.dma("sync", sb["y"][:, :], aps["oq"][t * 128:(t + 1) * 128, :], "lds")
            em.dma("sync", sb["sq"][:, :], aps["xres"][t * 128:(t + 1) * 128, :], "lds")
            em.wait("vector", "lds")
            em.op("vector", lambda v: v.tensor_add(sb["y"][:, :], sb["y"][:, :], sb["sq"][:, :]), "ve")
            if has_b:
                em.op("vector", lambda v: v.tensor_add(sb["y"][:, :], sb["y"][:, :], sb["brep"][:, :]), "ve")

            def fin(t=t):
                em.wait("gpsimd", "ve")
                em.dma("gpsimd", aps["xo"][t * 128:(t + 1) * 128, :], sb["sq"][:, :], "stg")
                em.wait("scalar", "ve")
                em.op("scalar", lambda s: s.activation(out=sb["xb"][:, :], in_=sb["sq"][:, :],
                      func=AF.Copy), "se")
                em.wait("tensor", "se")
                em.wait("vector", "stg")
                emit_transpose_seq(em, [sb["xb"][:, j * 128:(j + 1) * 128] for j in range(32)],
                                   [sb["tstg"][:, j * 128:(j + 1) * 128] for j in range(32)],
                                   sb["pt0"], sb["pt1"], sb["idb"])
                em.wait("gpsimd", "ve")
                for j in range(32):
                    em.dma("gpsimd", aps["xoT"][j * 128:(j + 1) * 128, t * 128:(t + 1) * 128],
                           sb["tstg"][:, j * 128:(j + 1) * 128], "stg")
            emit_ln(em, sb["y"][:, :], D, sb["stt"], sb["sq"][:, :], fin)

    in_maps = []
    for c in range(NC):
        m = {"OT": OT_l[c], "wo": W, "xres": xres_l[c], "ident": ident}
        if has_b:
            m["bias"] = bias.reshape(1, -1)
        in_maps.append(m)
    res = run_launch(body, tensors, sbufs, psums, in_maps)
    return [r["xo"] for r in res], [r["xoT"] for r in res]


def launch_l0_ffn(x1T_l, x1_l, qwT, scl, zp, bias, ident):
    tensors = [
        ("x1T", (4096, RPC), BF16, "ExternalInput"),
        ("x1", (RPC, D), F32, "ExternalInput"),
        ("qwT", (4096, 4096), I32, "ExternalInput"),
        ("scl", (1, 1), F32, "ExternalInput"),
        ("zp", (1, 1), F32, "ExternalInput"),
        ("bias", (1, 4096), F32, "ExternalInput"),
        ("ident", (128, 128), F32, "ExternalInput"),
        ("x2", (RPC, D), F32, "ExternalOutput"),
        ("x2T", (4096, RPC), BF16, "ExternalOutput"),
        ("oq", (RPC, D), F32, "Internal"),
    ]
    sbufs = [
        ("wq", (128, 32 * 1024), BF16),
        ("idb", (128, 128), BF16),
        ("xt", (128, 32 * 128), BF16),
        ("xrow", (128, D), F32),
        ("xrb", (128, D), BF16),
        ("rsall", (128, 8), F32),
        ("es", (128, 1024), F32),
        ("y", (128, D), F32),
        ("sq", (128, D), F32),
        ("stt", (128, 16), F32),
        ("tstg", (128, 32 * 128), BF16),
        ("ones", (1, 128), F32),
        ("srow", (1, 4), F32),
        ("svec", (128, 4), F32),
        ("brow", (1, 4096), F32),
        ("brep", (128, 4096), F32),
        ("bt", (128, 1), F32),
    ]
    psums = [
        ("pq", (128, 1024), F32),
        ("pt0", (128, 1024), BF16),
        ("pt1", (128, 1024), BF16),
    ]

    def body(em, aps, sb):
        em.dma("gpsimd", sb["idb"][:, :], aps["ident"][:, :], "ldg")
        em.dma("sync", sb["srow"][:, 0:1], aps["scl"][:, :], "lds")
        em.dma("sync", sb["srow"][:, 1:2], aps["zp"][:, :], "lds")
        em.dma("sync", sb["brow"][:, :], aps["bias"][:, :], "lds")
        em.op("vector", lambda v: v.memset(sb["ones"][:, :], 1.0), "ve")
        em.wait("tensor", "lds")
        em.wait("tensor", "ve")
        em.op("tensor", lambda t: t.matmul(sb["pq"][:, 0:4], sb["ones"][:, :], sb["srow"][:, 0:4],
                                           start=True, stop=True), "mm")
        em.wait("vector", "mm")
        em.op("vector", lambda v: v.tensor_copy(out=sb["svec"][:, :], in_=sb["pq"][:, 0:4]), "ve")
        em.op("vector", lambda v: v.tensor_scalar(out=sb["svec"][:, 2:3], in0=sb["svec"][:, 1:2],
              scalar1=sb["svec"][:, 0:1], scalar2=None, op0=ALU.mult), "ve")
        em.op("vector", lambda v: v.tensor_scalar(out=sb["svec"][:, 2:3], in0=sb["svec"][:, 2:3],
              scalar1=-1.0, scalar2=None, op0=ALU.mult), "ve")
        emit_bcast(em, sb["ones"], sb["brow"], sb["brep"], 4096, sb["pq"])
        for t in range(8):
            em.wait("sync", "se")
            em.dma("sync", sb["xrow"][:, :], aps["x1"][t * 128:(t + 1) * 128, :], "lds")
            em.wait("scalar", "lds")
            em.op("scalar", lambda s: s.activation(out=sb["xrb"][:, :], in_=sb["xrow"][:, :], func=AF.Copy), "se")
            em.wait("vector", "se")
            em.op("vector", lambda v, t=t: v.reduce_sum(sb["rsall"][:, t:t + 1], sb["xrb"][:, :], axis=AX.X), "ve")
        for q in range(4):
            for j in range(32):
                em.wait("gpsimd", "mm")
                em.dma("gpsimd", sb["wq"][:, j * 1024:(j + 1) * 1024],
                       aps["qwT"][j * 128:(j + 1) * 128, q * 1024:(q + 1) * 1024], "ldg")
            for t in range(8):
                em.wait("sync", "mm")
                em.dma("sync", sb["xt"][:, :].rearrange("p (j t) -> p j t", t=128),
                       tile3(aps["x1T"])[:, :, t * 128:(t + 1) * 128], "lds")
                em.wait("vector", "se")
                em.op("vector", lambda v, t=t: v.tensor_scalar(out=sb["bt"][:, :], in0=sb["rsall"][:, t:t + 1],
                      scalar1=sb["svec"][:, 2:3], scalar2=None, op0=ALU.mult), "ve")
                em.wait("tensor", "lds")
                em.wait("tensor", "ldg")
                em.wait("tensor", "se")
                em.wait("tensor", "ve")
                for ncol in range(2):
                    for j in range(32):
                        em.op("tensor", lambda tt, ncol=ncol, j=j: tt.matmul(
                            sb["pq"][:, ncol * 512:(ncol + 1) * 512],
                            sb["xt"][:, j * 128:(j + 1) * 128],
                            sb["wq"][:, j * 1024 + ncol * 512: j * 1024 + (ncol + 1) * 512],
                            start=(j == 0), stop=(j == 31)), "mm")
                em.wait("scalar", "mm")
                em.wait("scalar", "ve")
                em.wait("scalar", "stg")
                em.op("scalar", lambda s: s.activation(out=sb["es"][:, :], in_=sb["pq"][:, :],
                      func=AF.Copy, scale=sb["svec"][:, 0:1]), "se")
                em.wait("vector", "se")
                em.wait("vector", "stg")
                em.op("vector", lambda v: v.tensor_scalar(out=sb["es"][:, :], in0=sb["es"][:, :],
                      scalar1=sb["bt"][:, :], scalar2=None, op0=ALU.add), "ve")
                em.wait("gpsimd", "ve")
                em.dma("gpsimd", aps["oq"][t * 128:(t + 1) * 128, q * 1024:(q + 1) * 1024], sb["es"][:, :], "stg")
        for t in range(8):
            em.wait("sync", "se")
            em.wait("sync", "stg")
            em.dma("sync", sb["y"][:, :], aps["oq"][t * 128:(t + 1) * 128, :], "lds")
            em.dma("sync", sb["sq"][:, :], aps["x1"][t * 128:(t + 1) * 128, :], "lds")
            em.wait("vector", "lds")
            em.op("vector", lambda v: v.tensor_add(sb["y"][:, :], sb["y"][:, :], sb["sq"][:, :]), "ve")
            em.op("vector", lambda v: v.tensor_add(sb["y"][:, :], sb["y"][:, :], sb["brep"][:, :]), "ve")

            def fin(t=t):
                em.wait("gpsimd", "ve")
                em.dma("gpsimd", aps["x2"][t * 128:(t + 1) * 128, :], sb["sq"][:, :], "stg")
                em.wait("scalar", "ve")
                em.op("scalar", lambda s: s.activation(out=sb["xrb"][:, :], in_=sb["sq"][:, :],
                      func=AF.Copy), "se")
                em.wait("tensor", "se")
                em.wait("vector", "stg")
                emit_transpose_seq(em, [sb["xrb"][:, j * 128:(j + 1) * 128] for j in range(32)],
                                   [sb["tstg"][:, j * 128:(j + 1) * 128] for j in range(32)],
                                   sb["pt0"], sb["pt1"], sb["idb"])
                em.wait("gpsimd", "ve")
                for j in range(32):
                    em.dma("gpsimd", aps["x2T"][j * 128:(j + 1) * 128, t * 128:(t + 1) * 128],
                           sb["tstg"][:, j * 128:(j + 1) * 128], "stg")
            emit_ln(em, sb["y"][:, :], D, sb["stt"], sb["sq"][:, :], fin)

    in_maps = [{"x1T": x1T_l[c], "x1": x1_l[c], "qwT": qwT,
                "scl": scl.reshape(1, 1), "zp": zp.reshape(1, 1),
                "bias": bias.reshape(1, -1), "ident": ident} for c in range(NC)]
    res = run_launch(body, tensors, sbufs, psums, in_maps)
    return [r["x2"] for r in res], [r["x2T"] for r in res]


def launch_l1_qkv(x2T_b, wq_l, wk_l, wv_l, bq_l, bk_l, bv_l, ident):
    SCL = float(128.0 ** -0.5)
    tensors = [
        ("xT", (4096, S), BF16, "ExternalInput"),
        ("wq", (4096, 1024), F32, "ExternalInput"),
        ("wk", (4096, 1024), F32, "ExternalInput"),
        ("wv", (4096, 1024), F32, "ExternalInput"),
        ("bq", (1, 1024), F32, "ExternalInput"),
        ("bk", (1, 1024), F32, "ExternalInput"),
        ("bv", (1, 1024), F32, "ExternalInput"),
        ("ident", (128, 128), F32, "ExternalInput"),
        ("qT8", (1024, S), BF16, "ExternalOutput"),
        ("kT8", (1024, S), BF16, "ExternalOutput"),
        ("v8", (S, 1024), BF16, "ExternalOutput"),
    ]
    sbufs = [
        ("wp", (128, 32 * 1024), BF16),
        ("idb", (128, 128), BF16),
        ("xt", (128, 32 * 128), BF16),
        ("yb", (128, 1024), BF16),
        ("ones", (1, 128), F32),
        ("brow", (1, 1024), F32),
        ("brep", (128, 1024), F32),
        ("tstg", (128, 8 * 128), BF16),
    ]
    psums = [
        ("pq", (128, 1024), F32),
        ("pt0", (128, 1024), BF16),
        ("pt1", (128, 1024), BF16),
    ]

    def body(em, aps, sb):
        em.dma("gpsimd", sb["idb"][:, :], aps["ident"][:, :], "ldg")
        em.op("vector", lambda v: v.memset(sb["ones"][:, :], 1.0), "ve")
        for ph, (wn, bn, on) in enumerate((("wq", "bq", "qT8"), ("wk", "bk", "kT8"), ("wv", "bv", "v8"))):
            for j in range(32):
                em.wait("gpsimd", "mm")
                em.dma("gpsimd", sb["wp"][:, j * 1024:(j + 1) * 1024],
                       aps[wn][j * 128:(j + 1) * 128, :], "ldg")
            em.wait("sync", "mm")
            em.dma("sync", sb["brow"][:, :], aps[bn][:, :], "lds")
            em.wait("tensor", "lds")
            em.wait("tensor", "ve")
            emit_bcast(em, sb["ones"], sb["brow"], sb["brep"], 1024, sb["pq"])
            if ph == 0:
                em.op("vector", lambda v: v.tensor_scalar(out=sb["brep"][:, :], in0=sb["brep"][:, :],
                      scalar1=SCL, scalar2=None, op0=ALU.mult), "ve")
            for t in range(32):
                em.wait("sync", "mm")
                em.dma("sync", sb["xt"][:, :].rearrange("p (j t) -> p j t", t=128),
                       tile3(aps["xT"])[:, :, t * 128:(t + 1) * 128], "lds")
                em.wait("tensor", "lds")
                em.wait("tensor", "ldg")
                em.wait("tensor", "ve")
                for ncol in range(2):
                    for j in range(32):
                        em.op("tensor", lambda tt, ncol=ncol, j=j: tt.matmul(
                            sb["pq"][:, ncol * 512:(ncol + 1) * 512],
                            sb["xt"][:, j * 128:(j + 1) * 128],
                            sb["wp"][:, j * 1024 + ncol * 512: j * 1024 + (ncol + 1) * 512],
                            start=(j == 0), stop=(j == 31)), "mm")
                em.wait("vector", "mm")
                em.wait("vector", "stg")
                em.op("vector", lambda v, ph=ph: v.scalar_tensor_tensor(
                    out=sb["yb"][:, :], in0=sb["pq"][:, :], scalar=(SCL if ph == 0 else 1.0),
                    in1=sb["brep"][:, :], op0=ALU.mult, op1=ALU.add), "ve")
                if ph == 2:
                    em.wait("gpsimd", "ve")
                    em.dma("gpsimd", aps["v8"][t * 128:(t + 1) * 128, :], sb["yb"][:, :], "stg")
                else:
                    em.wait("vector", "stg")
                    emit_transpose_seq(em, [sb["yb"][:, j * 128:(j + 1) * 128] for j in range(8)],
                                       [sb["tstg"][:, j * 128:(j + 1) * 128] for j in range(8)],
                                       sb["pt0"], sb["pt1"], sb["idb"])
                    em.wait("gpsimd", "ve")
                    for j in range(8):
                        em.dma("gpsimd", aps[on][j * 128:(j + 1) * 128, t * 128:(t + 1) * 128],
                               sb["tstg"][:, j * 128:(j + 1) * 128], "stg")

    in_maps = [{"xT": x2T_b[c // 4], "wq": wq_l[c], "wk": wk_l[c], "wv": wv_l[c],
                "bq": bq_l[c], "bk": bk_l[c], "bv": bv_l[c], "ident": ident} for c in range(NC)]
    res = run_launch(body, tensors, sbufs, psums, in_maps)
    return [r["qT8"] for r in res], [r["kT8"] for r in res], [r["v8"] for r in res]


def launch_l1_ffn(x3T_l, x3_l, WT, bias, At, B2, coeffs, scales, ident):
    tensors = [
        ("x3T", (4096, RPC), BF16, "ExternalInput"),
        ("x3", (RPC, D), F32, "ExternalInput"),
        ("wT", (4096, 4096), F32, "ExternalInput"),
        ("bias", (1, 4096), F32, "ExternalInput"),
        ("At", (4096, 256), F32, "ExternalInput"),
        ("B2", (256, 4096), F32, "ExternalInput"),
        ("cf", (1, 16), F32, "ExternalInput"),
        ("sc", (1, 16), F32, "ExternalInput"),
        ("ident", (128, 128), F32, "ExternalInput"),
        ("out", (RPC, D), F32, "ExternalOutput"),
        ("qscr", (RPC, D), F32, "Internal"),
        ("uscr", (RPC, D), F32, "Internal"),
    ]
    sbufs = [
        ("wq", (128, 32 * 1024), BF16),
        ("idb", (128, 128), BF16),
        ("xt", (128, 32 * 128), BF16),
        ("xa", (128, 256), F32),
        ("xab", (128, 256), BF16),
        ("xaT", (128, 2 * 128), BF16),
        ("es", (128, 1024), F32),
        ("upd", (128, 4096), F32),
        ("y", (128, D), F32),
        ("sq", (128, D), F32),
        ("stt", (128, 16), F32),
        ("ones", (1, 128), F32),
        ("scrow", (1, 16), F32),
        ("w16row", (1, 16), F32),
        ("w16r", (128, 16), F32),
        ("brow", (1, 4096), F32),
        ("brep", (128, 4096), F32),
    ]
    psums = [
        ("pq", (128, 1024), F32),
        ("pu", (128, 512), F32),
        ("pt0", (128, 1024), BF16),
        ("pbc", (128, 16), F32),
    ]

    def body(em, aps, sb):
        at = sb["wq"][:, 0:32 * 256]
        b2 = sb["wq"][:, 32 * 256:32 * 256 + 2 * 4096]
        em.dma("gpsimd", sb["idb"][:, :], aps["ident"][:, :], "ldg")
        em.dma("sync", sb["w16row"][:, :], aps["cf"][:, :], "lds")
        em.dma("sync", sb["scrow"][:, :], aps["sc"][:, :], "lds")
        em.dma("sync", sb["brow"][:, :], aps["bias"][:, :], "lds")
        em.op("vector", lambda v: v.memset(sb["ones"][:, :], 1.0), "ve")
        em.wait("vector", "lds")
        em.op("vector", lambda v: v.tensor_mul(sb["w16row"][:, :], sb["w16row"][:, :], sb["scrow"][:, :]), "ve")
        em.wait("tensor", "ve")
        em.op("tensor", lambda t: t.matmul(sb["pbc"][:, :], sb["ones"][:, :], sb["w16row"][:, :],
                                           start=True, stop=True), "mm")
        em.wait("vector", "mm")
        em.op("vector", lambda v: v.tensor_copy(out=sb["w16r"][:, :], in_=sb["pbc"][:, :]), "ve")
        emit_bcast(em, sb["ones"], sb["brow"], sb["brep"], 4096, sb["pq"])
        for j in range(32):
            em.dma("gpsimd", at[:, j * 256:(j + 1) * 256], aps["At"][j * 128:(j + 1) * 128, :], "ldg")
        for j in range(2):
            em.dma("gpsimd", b2[:, j * 4096:(j + 1) * 4096], aps["B2"][j * 128:(j + 1) * 128, :], "ldg")
        em.wait("tensor", "ldg")
        for t in range(8):
            em.wait("sync", "mm")
            em.dma("sync", sb["xt"][:, :].rearrange("p (j t) -> p j t", t=128),
                   tile3(aps["x3T"])[:, :, t * 128:(t + 1) * 128], "lds")
            em.wait("tensor", "lds")
            em.wait("tensor", "ve")
            em.wait("tensor", "se")
            for j in range(32):
                em.op("tensor", lambda tt, j=j: tt.matmul(
                    sb["pq"][:, 0:256], sb["xt"][:, j * 128:(j + 1) * 128],
                    at[:, j * 256:(j + 1) * 256], start=(j == 0), stop=(j == 31)), "mm")
            em.wait("vector", "mm")
            em.wait("vector", "stg")
            em.op("vector", lambda v: v.tensor_copy(out=sb["xa"][:, :], in_=sb["pq"][:, 0:256]), "ve")
            for e in range(16):
                em.op("vector", lambda v, e=e: v.tensor_scalar(out=sb["xa"][:, e * 16:(e + 1) * 16],
                      in0=sb["xa"][:, e * 16:(e + 1) * 16], scalar1=sb["w16r"][:, e:e + 1],
                      scalar2=None, op0=ALU.mult), "ve")
            em.wait("scalar", "ve")
            em.op("scalar", lambda s: s.activation(out=sb["xab"][:, :], in_=sb["xa"][:, :], func=AF.Copy), "se")
            em.wait("tensor", "se")
            em.wait("tensor", "ve")
            for j in range(2):
                em.op("tensor", lambda tt, j=j: tt.transpose(sb["pt0"][:, j * 128:(j + 1) * 128],
                      sb["xab"][:, j * 128:(j + 1) * 128], sb["idb"][:, :]), "mm")
            em.wait("vector", "mm")
            em.op("vector", lambda v: v.tensor_copy(out=sb["xaT"][:, :], in_=sb["pt0"][:, 0:256]), "ve")
            em.wait("tensor", "ve")
            for ncol in range(8):
                for j in range(2):
                    em.op("tensor", lambda tt, ncol=ncol, j=j: tt.matmul(
                        sb["pu"][:, :], sb["xaT"][:, j * 128:(j + 1) * 128],
                        b2[:, j * 4096 + ncol * 512: j * 4096 + (ncol + 1) * 512],
                        start=(j == 0), stop=(j == 1)), "mm")
                em.wait("vector", "mm")
                em.op("vector", lambda v, ncol=ncol: v.tensor_copy(out=sb["upd"][:, ncol * 512:(ncol + 1) * 512],
                      in_=sb["pu"][:, :]), "ve")
                em.wait("tensor", "ve")
            em.wait("gpsimd", "ve")
            em.wait("gpsimd", "stg")
            em.dma("gpsimd", aps["uscr"][t * 128:(t + 1) * 128, :], sb["upd"][:, :], "stg")
        for q in range(4):
            for j in range(32):
                em.wait("gpsimd", "mm")
                em.dma("gpsimd", sb["wq"][:, j * 1024:(j + 1) * 1024],
                       aps["wT"][j * 128:(j + 1) * 128, q * 1024:(q + 1) * 1024], "ldg")
            for t in range(8):
                em.wait("sync", "mm")
                em.dma("sync", sb["xt"][:, :].rearrange("p (j t) -> p j t", t=128),
                       tile3(aps["x3T"])[:, :, t * 128:(t + 1) * 128], "lds")
                em.wait("tensor", "lds")
                em.wait("tensor", "ldg")
                em.wait("tensor", "ve")
                for ncol in range(2):
                    for j in range(32):
                        em.op("tensor", lambda tt, ncol=ncol, j=j: tt.matmul(
                            sb["pq"][:, ncol * 512:(ncol + 1) * 512],
                            sb["xt"][:, j * 128:(j + 1) * 128],
                            sb["wq"][:, j * 1024 + ncol * 512: j * 1024 + (ncol + 1) * 512],
                            start=(j == 0), stop=(j == 31)), "mm")
                em.wait("vector", "mm")
                em.wait("vector", "stg")
                em.op("vector", lambda v: v.tensor_copy(out=sb["es"][:, :], in_=sb["pq"][:, :]), "ve")
                em.wait("gpsimd", "ve")
                em.dma("gpsimd", aps["qscr"][t * 128:(t + 1) * 128, q * 1024:(q + 1) * 1024], sb["es"][:, :], "stg")
        for t in range(8):
            em.wait("sync", "se")
            em.wait("sync", "stg")
            em.dma("sync", sb["y"][:, :], aps["qscr"][t * 128:(t + 1) * 128, :], "lds")
            em.dma("sync", sb["sq"][:, :], aps["uscr"][t * 128:(t + 1) * 128, :], "lds")
            em.wait("vector", "lds")
            em.op("vector", lambda v: v.tensor_add(sb["y"][:, :], sb["y"][:, :], sb["sq"][:, :]), "ve")
            em.wait("sync", "ve")
            em.dma("sync", sb["sq"][:, :], aps["x3"][t * 128:(t + 1) * 128, :], "lds")
            em.wait("vector", "lds")
            em.op("vector", lambda v: v.tensor_add(sb["y"][:, :], sb["y"][:, :], sb["sq"][:, :]), "ve")
            em.op("vector", lambda v: v.tensor_add(sb["y"][:, :], sb["y"][:, :], sb["brep"][:, :]), "ve")

            def fin(t=t):
                em.wait("gpsimd", "ve")
                em.dma("gpsimd", aps["out"][t * 128:(t + 1) * 128, :], sb["sq"][:, :], "stg")
            emit_ln(em, sb["y"][:, :], D, sb["stt"], sb["sq"][:, :], fin)

    in_maps = [{"x3T": x3T_l[c], "x3": x3_l[c], "wT": WT, "bias": bias.reshape(1, -1),
                "At": At, "B2": B2, "cf": coeffs.reshape(1, -1), "sc": scales.reshape(1, -1),
                "ident": ident} for c in range(NC)]
    res = run_launch(body, tensors, sbufs, psums, in_maps)
    return [r["out"] for r in res]



def _rope_tables():
    pos = np.arange(S, dtype=np.float32)
    inv = (1.0 / (10000.0 ** (np.arange(0, 64, 2, dtype=np.float32) / np.float32(64.0)))).astype(np.float32)
    fr = np.outer(pos, inv).astype(np.float32)
    emb = np.concatenate([fr, fr], axis=1)
    return np.cos(emb).astype(np.float32), np.sin(emb).astype(np.float32)


def _alibi_slopes():
    import math
    n = 32
    base = 2.0 ** (-(2.0 ** -(math.log2(n) - 3)))
    return np.power(base, np.arange(1, n + 1, dtype=np.float32)).astype(np.float32)


def kernel(L0_attn_hidden_states, L0_attn_Wq_a, L0_attn_q_a_ln_w, L0_attn_Wq_b,
           L0_attn_Wkv_a, L0_attn_kv_a_ln_w, L0_attn_Wkv_b, L0_attn_Wo,
           L0_ffn_qweight, L0_ffn_scale, L0_ffn_zero_point, L0_ffn_bias,
           L1_attn_w_qkv, L1_attn_b_qkv, L1_attn_w_out, L1_attn_b_out,
           L1_ffn_W, L1_ffn_bias, L1_ffn_A, L1_ffn_Bf, L1_ffn_coeffs, L1_ffn_scales):
    X = np.ascontiguousarray(np.asarray(L0_attn_hidden_states, np.float32).reshape(T, D))
    ident = np.eye(128, dtype=np.float32)
    tri = np.ascontiguousarray(np.triu(np.full((128, 128), -1e9, np.float32), 1))
    cos0, sin0 = _rope_tables()

    perm_pe = np.concatenate([np.arange(0, 64, 2), np.arange(1, 64, 2)])
    Wkva_p = np.ascontiguousarray(np.concatenate(
        [np.asarray(L0_attn_Wkv_a, np.float32)[:, :512],
         np.asarray(L0_attn_Wkv_a, np.float32)[:, 512 + perm_pe]], axis=1))
    QD = 192
    nope_cols = np.concatenate([np.arange(h * QD, h * QD + 128) for h in range(32)])
    pe_cols = np.concatenate([h * QD + 128 + perm_pe for h in range(32)])
    Wqb_p = np.ascontiguousarray(np.asarray(L0_attn_Wq_b, np.float32)[:, np.concatenate([nope_cols, pe_cols])])
    wkvb = np.asarray(L0_attn_Wkv_b, np.float32)
    kcols = np.concatenate([np.arange(h * 256, h * 256 + 128) for h in range(32)])
    vcols = np.concatenate([np.arange(h * 256 + 128, h * 256 + 256) for h in range(32)])
    Wkvb_p = np.ascontiguousarray(wkvb[:, np.concatenate([kcols, vcols])])

    qanT_l, cnT_l, kpeT_l = launch_l0_down(X, np.ascontiguousarray(np.asarray(L0_attn_Wq_a, np.float32)),
                                           np.asarray(L0_attn_q_a_ln_w, np.float32),
                                           Wkva_p, np.asarray(L0_attn_kv_a_ln_w, np.float32),
                                           cos0, sin0, ident)
    qT_l = launch_l0_qb(qanT_l, Wqb_p, cos0, sin0, ident)
    kT_l, v_l = launch_l0_kvup(cnT_l, Wkvb_p)

    def cat_cols(lst, b):
        return np.ascontiguousarray(np.concatenate([np.asarray(lst[b * 4 + i]) for i in range(4)], axis=1))

    def cat_rows(lst, b):
        return np.ascontiguousarray(np.concatenate([np.asarray(lst[b * 4 + i]) for i in range(4)], axis=0))

    qT_b = [cat_cols(qT_l, b) for b in range(2)]
    kT_b = [cat_cols(kT_l, b) for b in range(2)]
    v_b = [cat_rows(v_l, b) for b in range(2)]
    kpeT_b = [cat_cols(kpeT_l, b) for b in range(2)]

    qTn_in, qTp_in, kT_in, v_in, kpe_in = [], [], [], [], []
    for c in range(NC):
        b, g = c // 4, c % 4
        qTn_in.append(np.ascontiguousarray(qT_b[b][g * 1024:(g + 1) * 1024]))
        qTp_in.append(np.ascontiguousarray(qT_b[b][4096 + g * 512: 4096 + (g + 1) * 512]))
        kT_in.append(np.ascontiguousarray(kT_b[b][g * 1024:(g + 1) * 1024]))
        v_in.append(np.ascontiguousarray(v_b[b][:, g * 1024:(g + 1) * 1024]))
        kpe_in.append(kpeT_b[b])
    OT8_l = launch_attn("l0", qTn_in, kT_in, v_in, qTp_l=qTp_in, kpeT_l=kpe_in, tri=tri)

    OT_b = [cat_rows(OT8_l, b) for b in range(2)]
    OT_in = [np.ascontiguousarray(OT_b[c // 4][:, (c % 4) * 1024:((c % 4) + 1) * 1024]) for c in range(NC)]
    xres_l = [np.ascontiguousarray(X[c * RPC:(c + 1) * RPC]) for c in range(NC)]
    x1_l, x1T_l = launch_oproj(OT_in, np.ascontiguousarray(np.asarray(L0_attn_Wo, np.float32)), None, xres_l, ident)

    qwT = np.ascontiguousarray(np.asarray(L0_ffn_qweight, np.int32).T)
    x2_l, x2T_l = launch_l0_ffn(x1T_l, x1_l, qwT, np.asarray(L0_ffn_scale, np.float32),
                                np.asarray(L0_ffn_zero_point, np.float32),
                                np.asarray(L0_ffn_bias, np.float32), ident)

    x2T_b = [cat_cols(x2T_l, b) for b in range(2)]
    wqkv = np.asarray(L1_attn_w_qkv, np.float32)
    bqkv = np.asarray(L1_attn_b_qkv, np.float32)
    wq_l, wk_l, wv_l, bq_l, bk_l, bv_l = [], [], [], [], [], []
    for c in range(NC):
        g = c % 4
        wq_l.append(np.ascontiguousarray(wqkv[:, g * 1024:(g + 1) * 1024]))
        wk_l.append(np.ascontiguousarray(wqkv[:, 4096 + g * 1024:4096 + (g + 1) * 1024]))
        wv_l.append(np.ascontiguousarray(wqkv[:, 8192 + g * 1024:8192 + (g + 1) * 1024]))
        bq_l.append(np.ascontiguousarray(bqkv[g * 1024:(g + 1) * 1024].reshape(1, -1)))
        bk_l.append(np.ascontiguousarray(bqkv[4096 + g * 1024:4096 + (g + 1) * 1024].reshape(1, -1)))
        bv_l.append(np.ascontiguousarray(bqkv[8192 + g * 1024:8192 + (g + 1) * 1024].reshape(1, -1)))
    qT8_l, kT8_l, v8_l = launch_l1_qkv(x2T_b, wq_l, wk_l, wv_l, bq_l, bk_l, bv_l, ident)

    slopes = _alibi_slopes()
    slopes_l = [np.ascontiguousarray(slopes[(c % 4) * 8:((c % 4) + 1) * 8].reshape(1, 8)) for c in range(NC)]
    OT8b_l = launch_attn("l1", qT8_l, kT8_l, v8_l, tri=tri, slopes_l=slopes_l)

    OTb_b = [cat_rows(OT8b_l, b) for b in range(2)]
    OTb_in = [np.ascontiguousarray(OTb_b[c // 4][:, (c % 4) * 1024:((c % 4) + 1) * 1024]) for c in range(NC)]
    x3_l, x3T_l = launch_oproj(OTb_in, np.ascontiguousarray(np.asarray(L1_attn_w_out, np.float32)),
                               np.asarray(L1_attn_b_out, np.float32), x2_l, ident)

    WT = np.ascontiguousarray(np.asarray(L1_ffn_W, np.float32).T)
    At = np.ascontiguousarray(np.asarray(L1_ffn_A, np.float32).transpose(2, 0, 1).reshape(4096, 256))
    B2 = np.ascontiguousarray(np.asarray(L1_ffn_Bf, np.float32).transpose(0, 2, 1).reshape(256, 4096))
    out_l = launch_l1_ffn(x3T_l, x3_l, WT, np.asarray(L1_ffn_bias, np.float32),
                          At, B2, np.asarray(L1_ffn_coeffs, np.float32),
                          np.asarray(L1_ffn_scales, np.float32), ident)

    out = np.concatenate([np.asarray(o, np.float32) for o in out_l], axis=0).reshape(2, S, D)
    return out


# version 05
# speedup vs baseline: 1.2707x; 1.2707x over previous
import numpy as np
import ml_dtypes
import contextlib
import concourse.bass as bass
import concourse.mybir as mybir
from concourse.bass_utils import run_bass_kernel_spmd

F32 = mybir.dt.float32
BF16 = mybir.dt.bfloat16
I32 = mybir.dt.int32
AF = mybir.ActivationFunctionType
ALU = mybir.AluOpType
AX = mybir.AxisListType
BF = ml_dtypes.bfloat16
NC = 8
D = 4096
T = 8192
RPC = 1024
S = 4096

SEMS = ("lds", "ldg", "mm", "ve", "se", "stg")


class Em:

    def __init__(self, sems, cur, eng):
        self.sems = sems
        self.cur = cur
        self.eng = eng
        self.cnt = {k: 0 for k in sems}

    def op(self, ename, fn, inc=None, k=1):
        if self.cur == ename and ename in ("vector", "scalar"):
            sem = "ve" if ename == "vector" else "se"
            self.eng.wait_ge(self.sems[sem], self.cnt[sem])
        if self.cur == ename:
            ins = fn(self.eng)
            if inc is not None:
                ins.then_inc(self.sems[inc], k)
        if inc is not None:
            self.cnt[inc] += k

    def dma(self, ename, out, in_, sem):
        self.op(ename, lambda e: e.dma_start(out=out, in_=in_), sem, 16)

    def wait(self, ename, sem, n=None):
        if self.cur == ename:
            self.eng.wait_ge(self.sems[sem], self.cnt[sem] if n is None else n)


def run_launch(body, tensors, sbufs, psums, in_maps):
    nc = bass.Bass("TRN2", target_bir_lowering=False)
    aps = {}
    for (name, shape, dt, kind) in tensors:
        aps[name] = nc.dram_tensor(name, list(shape), dt, kind=kind).ap()
    with contextlib.ExitStack() as st:
        sb = {}
        for (name, shape, dt) in sbufs:
            sb[name] = st.enter_context(nc.sbuf_tensor(name, list(shape), dt))
        for (name, shape, dt) in psums:
            sb[name] = st.enter_context(nc.psum_tensor(name, list(shape), dt))
        sems = {s: st.enter_context(nc.semaphore(s)) for s in SEMS}
        block = st.enter_context(nc.Block())
        for ename in ("sync", "gpsimd", "tensor", "vector", "scalar"):
            def f(eng, ename=ename):
                em = Em(sems, ename, eng)
                body(em, aps, sb)
                em.wait("gpsimd", "stg")
            getattr(block, ename)(f)
    res = run_bass_kernel_spmd(nc, in_maps, core_ids=list(range(NC)))
    return res.results


def tile3(ap, p=128):
    return ap.rearrange("(j p) t -> p j t", p=p)


def emit_transpose_seq(em, srcs, dsts, pt0, pt1, idb):
    n = len(srcs)
    G = 8
    gi = 0
    for s0 in range(0, n, G):
        grp = range(s0, min(s0 + G, n))
        pt = pt0 if gi % 2 == 0 else pt1
        em.wait("tensor", "ve")
        for k, idx in enumerate(grp):
            em.op("tensor", lambda t, k=k, idx=idx, pt=pt: t.transpose(
                pt[:, k * 128:(k + 1) * 128], srcs[idx], idb[:, :]), "mm")
        em.wait("vector", "mm")
        for k, idx in enumerate(grp):
            em.op("vector", lambda v, k=k, idx=idx, pt=pt: v.tensor_copy(
                out=dsts[idx], in_=pt[:, k * 128:(k + 1) * 128]), "ve")
        gi += 1


def emit_transpose_seq_g(em, src, dst, n, pt0, pt1, idb):
    G = 8
    gi = 0
    for s0 in range(0, n, G):
        ntk = min(G, n - s0)
        pt = pt0 if gi % 2 == 0 else pt1
        em.wait("tensor", "ve")
        for k in range(ntk):
            em.op("tensor", lambda t, k=k, s0=s0, pt=pt: t.transpose(
                pt[:, k * 128:(k + 1) * 128], src[:, (s0 + k) * 128:(s0 + k + 1) * 128],
                idb[:, :]), "mm")
        em.wait("vector", "mm")
        em.op("vector", lambda v, s0=s0, ntk=ntk, pt=pt: v.tensor_copy(
            out=dst[:, s0 * 128:(s0 + ntk) * 128], in_=pt[:, 0:ntk * 128]), "ve")
        gi += 1


def emit_bcast(em, ones, row, dst, W, ps):
    for i in range(0, W, 512):
        n = min(512, W - i)
        em.wait("tensor", "ve")
        em.op("tensor", lambda t, i=i, n=n: t.matmul(
            ps[:, 0:n], ones[:, :], row[:, i:i + n], start=True, stop=True), "mm")
        em.wait("vector", "mm")
        em.op("vector", lambda v, i=i, n=n: v.tensor_copy(out=dst[:, i:i + n], in_=ps[:, 0:n]), "ve")


def emit_rstd(em, y_ap, W, eps, stt, sq_ap):
    em.op("scalar", lambda s: s.activation(out=sq_ap, in_=y_ap, func=AF.Square), "se")
    em.wait("vector", "se")
    em.op("vector", lambda v: v.reduce_sum(stt[:, 0:1], sq_ap, axis=AX.X), "ve")
    em.op("vector", lambda v: v.tensor_scalar(out=stt[:, 1:2], in0=stt[:, 0:1],
          scalar1=1.0 / W, scalar2=None, op0=ALU.mult), "ve")
    em.op("vector", lambda v: v.tensor_scalar(out=stt[:, 1:2], in0=stt[:, 1:2],
          scalar1=eps, scalar2=None, op0=ALU.add), "ve")
    em.wait("scalar", "ve")
    em.op("scalar", lambda s: s.sqrt(stt[:, 2:3], stt[:, 1:2]), "se")
    em.wait("vector", "se")
    em.op("vector", lambda v: v.reciprocal(stt[:, 3:4], stt[:, 2:3]), "ve")


def emit_ln(em, y, W, stt, sq, out_emit):
    em.op("vector", lambda v: v.reduce_sum(stt[:, 0:1], y, axis=AX.X), "ve")
    em.op("vector", lambda v: v.tensor_scalar(out=stt[:, 1:2], in0=stt[:, 0:1],
          scalar1=1.0 / W, scalar2=None, op0=ALU.mult), "ve")
    em.op("vector", lambda v: v.tensor_mul(sq, y, y), "ve")
    em.op("vector", lambda v: v.reduce_sum(stt[:, 2:3], sq, axis=AX.X), "ve")
    em.op("vector", lambda v: v.tensor_mul(stt[:, 3:4], stt[:, 1:2], stt[:, 1:2]), "ve")
    em.op("vector", lambda v: v.tensor_scalar(out=stt[:, 4:5], in0=stt[:, 2:3],
          scalar1=1.0 / W, scalar2=None, op0=ALU.mult), "ve")
    em.op("vector", lambda v: v.tensor_sub(stt[:, 5:6], stt[:, 4:5], stt[:, 3:4]), "ve")
    em.op("vector", lambda v: v.tensor_scalar(out=stt[:, 5:6], in0=stt[:, 5:6],
          scalar1=1e-5, scalar2=None, op0=ALU.add), "ve")
    em.wait("scalar", "ve")
    em.op("scalar", lambda s: s.sqrt(stt[:, 6:7], stt[:, 5:6]), "se")
    em.wait("vector", "se")
    em.op("vector", lambda v: v.reciprocal(stt[:, 7:8], stt[:, 6:7]), "ve")
    em.op("vector", lambda v: v.tensor_scalar(out=stt[:, 8:9], in0=stt[:, 1:2],
          scalar1=stt[:, 7:8], scalar2=None, op0=ALU.mult), "ve")
    em.op("vector", lambda v: v.tensor_scalar(out=stt[:, 8:9], in0=stt[:, 8:9],
          scalar1=-1.0, scalar2=None, op0=ALU.mult), "ve")
    em.wait("scalar", "ve")
    em.op("scalar", lambda s: s.activation(out=sq, in_=y, func=AF.Copy, scale=stt[:, 7:8]), "se")
    em.wait("vector", "se")
    em.op("vector", lambda v: v.tensor_scalar(out=sq, in0=sq, scalar1=stt[:, 8:9],
          scalar2=None, op0=ALU.add), "ve")
    out_emit()



def launch_l0_down(X, Wqa, lnq, Wkva_p, lnkv, cos0, sin0, ident):
    KW = 1536
    tensors = [
        ("x", (RPC, D), F32, "ExternalInput"),
        ("wqa", (D, KW), F32, "ExternalInput"),
        ("lnq", (1, KW), F32, "ExternalInput"),
        ("wkva", (D, 576), F32, "ExternalInput"),
        ("lnkv", (1, 512), F32, "ExternalInput"),
        ("cos", (RPC, 64), F32, "ExternalInput"),
        ("sin", (RPC, 64), F32, "ExternalInput"),
        ("ident", (128, 128), F32, "ExternalInput"),
        ("qanT", (KW, RPC), BF16, "ExternalOutput"),
        ("cnT", (512, RPC), BF16, "ExternalOutput"),
        ("kpeT", (64, RPC), BF16, "ExternalOutput"),
    ]
    sbufs = [
        ("wq", (128, 32 * KW), BF16),
        ("wkv", (128, 32 * 576), BF16),
        ("lnqr", (128, KW), F32),
        ("lnkvr", (128, 512), F32),
        ("rowq", (1, KW), F32),
        ("rowkv", (1, 512), F32),
        ("ones", (1, 128), F32),
        ("idb", (128, 128), BF16),
        ("xbf", (128, D), BF16),
        ("xT", (128, 32 * 128), BF16),
        ("tmp", (128, KW), F32),
        ("stt", (128, 16), F32),
        ("qanb", (128, KW), BF16),
        ("cnb", (128, 512), BF16),
        ("cost", (128, 64), F32),
        ("sint", (128, 64), F32),
        ("pe", (128, 64), F32),
        ("rot", (128, 64), F32),
        ("t1", (128, 64), F32),
        ("kpb", (128, 64), BF16),
        ("tstg", (128, 17 * 128), BF16),
    ]
    psums = [
        ("pmm", (128, 1536), F32),
        ("pkv", (128, 1024), F32),
        ("pt0", (128, 1024), BF16),
        ("pt1", (128, 1024), BF16),
    ]

    def body(em, aps, sb):
        for j in range(32):
            em.dma("gpsimd", sb["wq"][:, j * KW:(j + 1) * KW], aps["wqa"][j * 128:(j + 1) * 128, :], "ldg")
            em.dma("gpsimd", sb["wkv"][:, j * 576:(j + 1) * 576], aps["wkva"][j * 128:(j + 1) * 128, :], "ldg")
        em.dma("gpsimd", sb["idb"][:, :], aps["ident"][:, :], "ldg")
        em.dma("sync", sb["rowq"][:, :], aps["lnq"][:, :], "lds")
        em.dma("sync", sb["rowkv"][:, :], aps["lnkv"][:, :], "lds")
        em.op("vector", lambda v: v.memset(sb["ones"][:, :], 1.0), "ve")
        em.wait("tensor", "lds")
        em.wait("tensor", "ve")
        emit_bcast(em, sb["ones"], sb["rowq"], sb["lnqr"], KW, sb["pmm"])
        emit_bcast(em, sb["ones"], sb["rowkv"], sb["lnkvr"], 512, sb["pkv"])
        for t in range(8):
            em.wait("gpsimd", "mm")
            em.dma("gpsimd", sb["xbf"][:, :], aps["x"][t * 128:(t + 1) * 128, :], "ldg")
            em.wait("sync", "ve")
            em.dma("sync", sb["cost"][:, :], aps["cos"][t * 128:(t + 1) * 128, :], "lds")
            em.dma("sync", sb["sint"][:, :], aps["sin"][t * 128:(t + 1) * 128, :], "lds")
            em.wait("tensor", "ldg")
            em.wait("tensor", "se")
            emit_transpose_seq(em, [sb["xbf"][:, j * 128:(j + 1) * 128] for j in range(32)],
                               [sb["xT"][:, j * 128:(j + 1) * 128] for j in range(32)],
                               sb["pt0"], sb["pt1"], sb["idb"])
            em.wait("tensor", "ve")
            for ncol in range(3):
                for j in range(32):
                    em.op("tensor", lambda tt, ncol=ncol, j=j: tt.matmul(
                        sb["pmm"][:, ncol * 512:(ncol + 1) * 512],
                        sb["xT"][:, j * 128:(j + 1) * 128],
                        sb["wq"][:, j * KW + ncol * 512: j * KW + (ncol + 1) * 512],
                        start=(j == 0), stop=(j == 31)), "mm")
            for (n0, n1) in ((0, 512), (512, 576)):
                for j in range(32):
                    em.op("tensor", lambda tt, n0=n0, n1=n1, j=j: tt.matmul(
                        sb["pkv"][:, n0:n1],
                        sb["xT"][:, j * 128:(j + 1) * 128],
                        sb["wkv"][:, j * 576 + n0: j * 576 + n1],
                        start=(j == 0), stop=(j == 31)), "mm")
            em.wait("vector", "mm")
            em.wait("scalar", "mm")
            em.wait("vector", "stg")
            emit_rstd(em, sb["pmm"][:, 0:KW], KW, 1e-6, sb["stt"], sb["tmp"][:, :])
            em.wait("scalar", "ve")
            em.op("scalar", lambda s: s.activation(out=sb["tmp"][:, :], in_=sb["pmm"][:, 0:KW],
                  func=AF.Copy, scale=sb["stt"][:, 3:4]), "se")
            em.wait("vector", "se")
            em.op("vector", lambda v: v.tensor_mul(sb["qanb"][:, :], sb["tmp"][:, :], sb["lnqr"][:, :]), "ve")
            em.wait("scalar", "ve")
            emit_rstd(em, sb["pkv"][:, 0:512], 512, 1e-6, sb["stt"], sb["tmp"][:, 0:512])
            em.wait("scalar", "ve")
            em.op("scalar", lambda s: s.activation(out=sb["tmp"][:, 0:512], in_=sb["pkv"][:, 0:512],
                  func=AF.Copy, scale=sb["stt"][:, 3:4]), "se")
            em.wait("vector", "se")
            em.op("vector", lambda v: v.tensor_mul(sb["cnb"][:, :], sb["tmp"][:, 0:512], sb["lnkvr"][:, :]), "ve")
            em.wait("vector", "lds")
            em.op("vector", lambda v: v.tensor_copy(out=sb["pe"][:, :], in_=sb["pkv"][:, 512:576]), "ve")
            em.op("vector", lambda v: v.tensor_mul(sb["t1"][:, :], sb["pe"][:, :], sb["cost"][:, :]), "ve")
            em.op("vector", lambda v: v.tensor_scalar(out=sb["rot"][:, 0:32], in0=sb["pe"][:, 32:64],
                  scalar1=-1.0, scalar2=None, op0=ALU.mult), "ve")
            em.op("vector", lambda v: v.tensor_copy(out=sb["rot"][:, 32:64], in_=sb["pe"][:, 0:32]), "ve")
            em.op("vector", lambda v: v.tensor_mul(sb["rot"][:, :], sb["rot"][:, :], sb["sint"][:, :]), "ve")
            em.op("vector", lambda v: v.tensor_add(sb["kpb"][:, :], sb["t1"][:, :], sb["rot"][:, :]), "ve")
            srcs = [sb["qanb"][:, j * 128:(j + 1) * 128] for j in range(12)] + \
                   [sb["cnb"][:, j * 128:(j + 1) * 128] for j in range(4)]
            dsts = [sb["tstg"][:, j * 128:(j + 1) * 128] for j in range(16)]
            emit_transpose_seq(em, srcs, dsts, sb["pt0"], sb["pt1"], sb["idb"])
            em.wait("tensor", "ve")
            em.op("tensor", lambda tt: tt.transpose(sb["pt0"][0:64, 0:128], sb["kpb"][:, :], sb["idb"][:, :]), "mm")
            em.wait("vector", "mm")
            em.op("vector", lambda v: v.tensor_copy(out=sb["tstg"][0:64, 16 * 128:17 * 128],
                  in_=sb["pt0"][0:64, 0:128]), "ve")
            em.wait("gpsimd", "ve")
            for j in range(12):
                em.dma("gpsimd", aps["qanT"][j * 128:(j + 1) * 128, t * 128:(t + 1) * 128],
                       sb["tstg"][:, j * 128:(j + 1) * 128], "stg")
            for j in range(4):
                em.dma("gpsimd", aps["cnT"][j * 128:(j + 1) * 128, t * 128:(t + 1) * 128],
                       sb["tstg"][:, (12 + j) * 128:(13 + j) * 128], "stg")
            em.dma("gpsimd", aps["kpeT"][0:64, t * 128:(t + 1) * 128],
                   sb["tstg"][0:64, 16 * 128:17 * 128], "stg")

    in_maps = []
    for c in range(NC):
        p0 = (c % 4) * 1024
        in_maps.append({
            "x": X[c * RPC:(c + 1) * RPC], "wqa": Wqa, "lnq": lnq.reshape(1, -1),
            "wkva": Wkva_p, "lnkv": lnkv.reshape(1, -1),
            "cos": cos0[p0:p0 + RPC], "sin": sin0[p0:p0 + RPC], "ident": ident,
        })
    res = run_launch(body, tensors, sbufs, psums, in_maps)
    return ([r["qanT"] for r in res], [r["cnT"] for r in res], [r["kpeT"] for r in res])


def launch_l0_qb(qanT_l, Wqb_p, cos0, sin0, ident):
    KW = 1536
    SCL = float(192.0 ** -0.5)
    tensors = [
        ("qanT", (KW, RPC), BF16, "ExternalInput"),
        ("wqb", (KW, 6144), F32, "ExternalInput"),
        ("cos", (RPC, 64), F32, "ExternalInput"),
        ("sin", (RPC, 64), F32, "ExternalInput"),
        ("ident", (128, 128), F32, "ExternalInput"),
        ("qT", (6144, RPC), BF16, "ExternalOutput"),
    ]
    sbufs = [
        ("wq", (128, 12 * 3072), BF16),
        ("idb", (128, 128), BF16),
        ("qat", (128, 12 * 128), BF16),
        ("q2b", (128, 3072), BF16),
        ("pef", (128, 2048), F32),
        ("t1", (128, 2048), F32),
        ("rot", (128, 2048), F32),
        ("cost", (128, 64), F32),
        ("sint", (128, 64), F32),
        ("tstg", (128, 24 * 128), BF16),
    ]
    psums = [
        ("pmm", (128, 3072), F32),
        ("pt0", (128, 1024), BF16),
        ("pt1", (128, 1024), BF16),
    ]

    def body(em, aps, sb):
        em.dma("gpsimd", sb["idb"][:, :], aps["ident"][:, :], "ldg")
        for ph in range(2):
            c0 = ph * 3072
            for j in range(12):
                em.wait("gpsimd", "mm")
                em.dma("gpsimd", sb["wq"][:, j * 3072:(j + 1) * 3072],
                       aps["wqb"][j * 128:(j + 1) * 128, c0:c0 + 3072], "ldg")
            for t in range(8):
                em.wait("sync", "mm")
                em.wait("sync", "ve")
                em.dma("sync", sb["qat"][:, :].rearrange("p (j t) -> p j t", t=128),
                       tile3(aps["qanT"])[:, :, t * 128:(t + 1) * 128], "lds")
                em.dma("sync", sb["cost"][:, :], aps["cos"][t * 128:(t + 1) * 128, :], "lds")
                em.dma("sync", sb["sint"][:, :], aps["sin"][t * 128:(t + 1) * 128, :], "lds")
                em.wait("tensor", "lds")
                em.wait("tensor", "ldg")
                em.wait("tensor", "se")
                em.wait("tensor", "ve")
                for ncol in range(6):
                    for j in range(12):
                        em.op("tensor", lambda tt, ncol=ncol, j=j: tt.matmul(
                            sb["pmm"][:, ncol * 512:(ncol + 1) * 512],
                            sb["qat"][:, j * 128:(j + 1) * 128],
                            sb["wq"][:, j * 3072 + ncol * 512: j * 3072 + (ncol + 1) * 512],
                            start=(j == 0), stop=(j == 11)), "mm")
                em.wait("scalar", "mm")
                em.wait("vector", "mm")
                if ph == 0:
                    em.op("scalar", lambda s: s.activation(out=sb["q2b"][:, :], in_=sb["pmm"][:, :],
                          func=AF.Copy, scale=SCL), "se")
                else:
                    em.op("scalar", lambda s: s.activation(out=sb["q2b"][:, 0:1024], in_=sb["pmm"][:, 0:1024],
                          func=AF.Copy, scale=SCL), "se")
                    em.op("vector", lambda v: v.tensor_copy(out=sb["pef"][:, :], in_=sb["pmm"][:, 1024:3072]), "ve")
                    pe3 = sb["pef"][:, :].rearrange("p (h f) -> p h f", f=64)
                    t13 = sb["t1"][:, :].rearrange("p (h f) -> p h f", f=64)
                    ro3 = sb["rot"][:, :].rearrange("p (h f) -> p h f", f=64)
                    cosb = sb["cost"][:, :].rearrange("p (o f) -> p o f", o=1)
                    sinb = sb["sint"][:, :].rearrange("p (o f) -> p o f", o=1)

                    def tt_b(v, outap, a, b):
                        a2, b2 = bass.broadcast_tensor_aps(a, b)
                        return v.tensor_tensor(out=outap, in0=a2, in1=b2, op=ALU.mult)
                    em.op("vector", lambda v: tt_b(v, t13, pe3, cosb), "ve")
                    em.op("vector", lambda v: v.tensor_scalar(out=ro3[:, :, 0:32], in0=pe3[:, :, 32:64],
                          scalar1=-1.0, scalar2=None, op0=ALU.mult), "ve")
                    em.op("vector", lambda v: v.tensor_copy(out=ro3[:, :, 32:64], in_=pe3[:, :, 0:32]), "ve")
                    em.op("vector", lambda v: tt_b(v, ro3, ro3, sinb), "ve")
                    em.op("vector", lambda v: v.tensor_add(sb["t1"][:, :], sb["t1"][:, :], sb["rot"][:, :]), "ve")
                    em.wait("scalar", "ve")
                    em.op("scalar", lambda s: s.activation(out=sb["q2b"][:, 1024:3072], in_=sb["t1"][:, :],
                          func=AF.Copy, scale=SCL), "se")
                em.wait("tensor", "se")
                em.wait("vector", "stg")
                emit_transpose_seq(em, [sb["q2b"][:, j * 128:(j + 1) * 128] for j in range(24)],
                                   [sb["tstg"][:, j * 128:(j + 1) * 128] for j in range(24)],
                                   sb["pt0"], sb["pt1"], sb["idb"])
                em.wait("gpsimd", "ve")
                for j in range(24):
                    em.dma("gpsimd", aps["qT"][c0 + j * 128:c0 + (j + 1) * 128, t * 128:(t + 1) * 128],
                           sb["tstg"][:, j * 128:(j + 1) * 128], "stg")

    in_maps = []
    for c in range(NC):
        p0 = (c % 4) * 1024
        in_maps.append({
            "qanT": qanT_l[c], "wqb": Wqb_p,
            "cos": cos0[p0:p0 + RPC], "sin": sin0[p0:p0 + RPC], "ident": ident,
        })
    res = run_launch(body, tensors, sbufs, psums, in_maps)
    return [r["qT"] for r in res]


def launch_l0_kvup(cnT_l, Wkvb_p):
    tensors = [
        ("cnT", (512, RPC), BF16, "ExternalInput"),
        ("wkvb", (512, 8192), F32, "ExternalInput"),
        ("kT", (4096, RPC), BF16, "ExternalOutput"),
        ("v", (RPC, 4096), BF16, "ExternalOutput"),
    ]
    sbufs = [
        ("wb", (128, 4 * 8192), BF16),
        ("cn", (128, 4 * 1024), BF16),
        ("es", (128, 1024), BF16),
    ]
    psums = [
        ("pk0", (128, 512), F32),
        ("pk1", (128, 512), F32),
    ]

    def body(em, aps, sb):
        for j in range(4):
            em.dma("gpsimd", sb["wb"][:, j * 8192:(j + 1) * 8192], aps["wkvb"][j * 128:(j + 1) * 128, :], "ldg")
            em.dma("sync", sb["cn"][:, j * 1024:(j + 1) * 1024], aps["cnT"][j * 128:(j + 1) * 128, :], "lds")
        em.wait("tensor", "ldg")
        em.wait("tensor", "lds")
        bi = 0

        def unit(mm_emit, outap):
            nonlocal bi
            ps = sb["pk0"] if bi % 2 == 0 else sb["pk1"]
            eslot = sb["es"][:, (bi % 2) * 512:(bi % 2) * 512 + 512]
            em.wait("tensor", "ve")
            em.wait("tensor", "stg" if False else "ve")
            mm_emit(ps)
            em.wait("vector", "mm")
            em.wait("vector", "stg")
            em.op("vector", lambda v, ps=ps, eslot=eslot: v.tensor_copy(out=eslot, in_=ps[:, :]), "ve")
            em.wait("gpsimd", "ve")
            em.dma("gpsimd", outap, eslot, "stg")
            bi += 1

        for h in range(32):
            for kc in range(2):
                def mk(ps, h=h, kc=kc):
                    for j in range(4):
                        em.op("tensor", lambda tt, j=j, h=h, kc=kc, ps=ps: tt.matmul(
                            ps[:, :], sb["wb"][:, j * 8192 + h * 128: j * 8192 + (h + 1) * 128],
                            sb["cn"][:, j * 1024 + kc * 512: j * 1024 + (kc + 1) * 512],
                            start=(j == 0), stop=(j == 3)), "mm")
                unit(mk, aps["kT"][h * 128:(h + 1) * 128, kc * 512:(kc + 1) * 512])
        for g in range(8):
            for kt in range(8):
                def mk(ps, g=g, kt=kt):
                    for j in range(4):
                        em.op("tensor", lambda tt, j=j, g=g, kt=kt, ps=ps: tt.matmul(
                            ps[:, :], sb["cn"][:, j * 1024 + kt * 128: j * 1024 + (kt + 1) * 128],
                            sb["wb"][:, j * 8192 + 4096 + g * 512: j * 8192 + 4096 + (g + 1) * 512],
                            start=(j == 0), stop=(j == 3)), "mm")
                unit(mk, aps["v"][kt * 128:(kt + 1) * 128, g * 512:(g + 1) * 512])

    in_maps = [{"cnT": cnT_l[c], "wkvb": Wkvb_p} for c in range(NC)]
    res = run_launch(body, tensors, sbufs, psums, in_maps)
    return [r["kT"] for r in res], [r["v"] for r in res]


def launch_attn(mode, qTn_l, kT_l, v_l, qTp_l=None, kpeT_l=None, tri=None, slopes_l=None):
    is_l0 = (mode == "l0")
    tensors = [
        ("qTn", (1024, S), BF16, "ExternalInput"),
        ("kT8", (1024, S), BF16, "ExternalInput"),
        ("v8", (S, 1024), BF16, "ExternalInput"),
        ("ident", (128, 128), F32, "ExternalInput"),
        ("tri", (128, 128), F32, "ExternalInput"),
        ("OT8", (1024, S), BF16, "ExternalOutput"),
    ]
    if is_l0:
        tensors += [
            ("qTp", (512, S), BF16, "ExternalInput"),
            ("kpeT", (64, S), BF16, "ExternalInput"),
        ]
    else:
        tensors += [
            ("slopes", (1, 8), F32, "ExternalInput"),
            ("jrow", (1, S), F32, "ExternalInput"),
        ]
    sbufs = [
        ("qh", (128, S), BF16),
        ("kh", (128, S), BF16),
        ("vh", (128, 32 * 128), BF16),
        ("ssb", (128, 2 * S), F32),
        ("pb", (128, S), BF16),
        ("pts", (128, 32 * 128), BF16),
        ("obf", (128, 128), BF16),
        ("stt", (128, 16), F32),
        ("idb", (128, 128), BF16),
        ("trib", (128, 128), F32),
    ]
    if is_l0:
        sbufs += [("qph", (64, S), BF16), ("kpe", (64, S), BF16)]
    else:
        sbufs += [("plane", (128, S), F32), ("prow", (1, S), F32), ("prow2", (1, S), F32),
                  ("srow", (1, 8), F32), ("ones", (1, 128), F32)]
    psums = [
        ("ps0", (128, 512), F32),
        ("ps1", (128, 512), F32),
        ("pt0", (128, 1024), BF16),
        ("pt1", (128, 1024), BF16),
        ("po", (128, 128), F32),
        ("pbc", (128, 512), F32),
    ]

    def body(em, aps, sb):
        em.dma("gpsimd", sb["idb"][:, :], aps["ident"][:, :], "ldg")
        em.dma("sync", sb["trib"][:, :], aps["tri"][:, :], "lds")
        if is_l0:
            em.dma("sync", sb["kpe"][0:64, :], aps["kpeT"][:, :], "lds")
        else:
            em.dma("sync", sb["srow"][:, :], aps["slopes"][:, :], "lds")
            em.dma("sync", sb["prow"][:, :], aps["jrow"][:, :], "lds")
            em.op("vector", lambda v: v.memset(sb["ones"][:, :], 1.0), "ve")
        state = {"tmark": 0}
        for hh in range(8):
            em.wait("sync", "mm")
            em.dma("sync", sb["qh"][:, :], aps["qTn"][hh * 128:(hh + 1) * 128, :], "lds")
            em.dma("sync", sb["kh"][:, :], aps["kT8"][hh * 128:(hh + 1) * 128, :], "lds")
            em.dma("sync", sb["vh"][:, :].rearrange("p (j t) -> p j t", t=128),
                   tile3(aps["v8"])[:, :, hh * 128:(hh + 1) * 128], "lds")
            if is_l0:
                em.dma("sync", sb["qph"][0:64, :], aps["qTp"][hh * 64:(hh + 1) * 64, :], "lds")
            em.wait("tensor", "lds")
            if not is_l0:
                em.wait("vector", "lds")
                em.wait("vector", "mm")
                em.op("vector", lambda v, hh=hh: v.tensor_scalar(out=sb["prow2"][:, :], in0=sb["prow"][:, :],
                      scalar1=sb["srow"][:, hh:hh + 1], scalar2=None, op0=ALU.mult), "ve")
                em.wait("tensor", "ve")
                for i in range(0, S, 512):
                    if i >= 1024:
                        em.wait("tensor", "ve")
                    em.op("tensor", lambda t, i=i: t.matmul(
                        sb["pbc"][:, 0:512] if (i // 512) % 2 == 0 else sb["ps0"][:, 0:512],
                        sb["ones"][:, :], sb["prow2"][:, i:i + 512],
                        start=True, stop=True), "mm")
                    em.wait("vector", "mm")
                    em.op("vector", lambda v, i=i: v.tensor_copy(
                        out=sb["plane"][:, i:i + 512],
                        in_=sb["pbc"][:, 0:512] if (i // 512) % 2 == 0 else sb["ps0"][:, 0:512]), "ve")
            marks = {}

            def s1(qt):
                nk = 128 * (qt + 1)
                nchunk = (nk + 511) // 512
                par = qt % 2
                sv = sb["ssb"][:, par * S: par * S + nk]
                csem = "se" if is_l0 else "ve"
                em.wait("tensor", csem)
                if not is_l0:
                    em.wait("vector", "se")
                for kc in range(nchunk):
                    n0 = kc * 512
                    n1 = min(nk, n0 + 512)
                    ps = sb["ps0"] if kc % 2 == 0 else sb["ps1"]
                    if kc >= 2:
                        em.wait("tensor", csem)
                    em.op("tensor", lambda t, ps=ps, n0=n0, n1=n1, qt=qt: t.matmul(
                        ps[:, 0:n1 - n0], sb["qh"][:, qt * 128:(qt + 1) * 128], sb["kh"][:, n0:n1],
                        start=True, stop=(not is_l0)), "mm")
                    if is_l0:
                        em.op("tensor", lambda t, ps=ps, n0=n0, n1=n1, qt=qt: t.matmul(
                            ps[:, 0:n1 - n0], sb["qph"][0:64, qt * 128:(qt + 1) * 128], sb["kpe"][0:64, n0:n1],
                            start=False, stop=True), "mm")
                    if is_l0:
                        em.wait("scalar", "mm")
                        em.op("scalar", lambda s, ps=ps, n0=n0, n1=n1, par=par: s.activation(
                            out=sb["ssb"][:, par * S + n0: par * S + n1], in_=ps[:, 0:n1 - n0],
                            func=AF.Copy), "se")
                    else:
                        em.wait("vector", "mm")
                        em.op("vector", lambda v, ps=ps, n0=n0, n1=n1, par=par: v.tensor_add(
                            sb["ssb"][:, par * S + n0: par * S + n1], sb["plane"][:, n0:n1],
                            ps[:, 0:n1 - n0]), "ve")
                marks[qt] = em.cnt["se" if is_l0 else "ve"]

            def s2(qt):
                nk = 128 * (qt + 1)
                ntile = qt + 1
                par = qt % 2
                sv = sb["ssb"][:, par * S: par * S + nk]
                sttv = sb["stt"][:, 8 * par: 8 * par + 8]
                if is_l0:
                    em.wait("vector", "se", marks[qt])
                em.op("vector", lambda v, nk=nk, sv=sv: v.tensor_add(sv[:, nk - 128:nk],
                      sv[:, nk - 128:nk], sb["trib"][:, :]), "ve")
                em.op("vector", lambda v, sv=sv: v.reduce_max(sttv[:, 0:1], sv, axis=AX.X), "ve")
                em.op("vector", lambda v: v.tensor_scalar(out=sttv[:, 1:2], in0=sttv[:, 0:1],
                      scalar1=-1.0, scalar2=None, op0=ALU.mult), "ve")
                em.wait("scalar", "ve")
                em.wait("scalar", "mm", state["tmark"])
                em.op("scalar", lambda s, nk=nk, sv=sv: s.activation(out=sb["pb"][:, 0:nk], in_=sv,
                      func=AF.Exp, bias=sttv[:, 1:2]), "se")
                em.wait("vector", "se")
                em.op("vector", lambda v, nk=nk: v.reduce_sum(sttv[:, 2:3], sb["pb"][:, 0:nk], axis=AX.X), "ve")
                em.op("vector", lambda v: v.reciprocal(sttv[:, 3:4], sttv[:, 2:3]), "ve")
                em.wait("scalar", "ve")
                em.op("scalar", lambda s, nk=nk: s.activation(out=sb["pb"][:, 0:nk], in_=sb["pb"][:, 0:nk],
                      func=AF.Copy, scale=sttv[:, 3:4]), "se")
                em.wait("tensor", "se")
                emit_transpose_seq_g(em, sb["pb"], sb["pts"], ntile,
                                     sb["pt0"], sb["pt1"], sb["idb"])
                em.wait("tensor", "ve")
                for kt in range(ntile):
                    em.op("tensor", lambda t, kt=kt, ntile=ntile: t.matmul(
                        sb["po"][:, :], sb["vh"][:, kt * 128:(kt + 1) * 128],
                        sb["pts"][:, kt * 128:(kt + 1) * 128],
                        start=(kt == 0), stop=(kt == ntile - 1)), "mm")
                state["tmark"] = em.cnt["mm"]
                em.wait("vector", "mm")
                em.wait("vector", "stg")
                em.op("vector", lambda v: v.tensor_copy(out=sb["obf"][:, :], in_=sb["po"][:, :]), "ve")
                em.wait("gpsimd", "ve")
                em.dma("gpsimd", aps["OT8"][hh * 128:(hh + 1) * 128, qt * 128:(qt + 1) * 128],
                       sb["obf"][:, :], "stg")

            s1(0)
            for qt in range(32):
                if qt + 1 < 32:
                    s1(qt + 1)
                s2(qt)

    in_maps = []
    for c in range(NC):
        m = {"qTn": qTn_l[c], "kT8": kT_l[c], "v8": v_l[c], "ident": np.eye(128, dtype=np.float32),
             "tri": tri}
        if is_l0:
            m["qTp"] = qTp_l[c]
            m["kpeT"] = kpeT_l[c]
        else:
            m["slopes"] = slopes_l[c]
            m["jrow"] = np.arange(S, dtype=np.float32).reshape(1, S)
        in_maps.append(m)
    res = run_launch(body, tensors, sbufs, psums, in_maps)
    return [r["OT8"] for r in res]


def launch_oproj(OT_l, W, bias, xres_l, ident):
    has_b = bias is not None
    tensors = [
        ("OT", (4096, RPC), BF16, "ExternalInput"),
        ("wo", (4096, 4096), F32, "ExternalInput"),
        ("xres", (RPC, D), F32, "ExternalInput"),
        ("ident", (128, 128), F32, "ExternalInput"),
        ("xo", (RPC, D), F32, "ExternalOutput"),
        ("xoT", (4096, RPC), BF16, "ExternalOutput"),
        ("oq", (RPC, D), F32, "Internal"),
    ]
    if has_b:
        tensors.append(("bias", (1, 4096), F32, "ExternalInput"))
    sbufs = [
        ("wq", (128, 32 * 1024), BF16),
        ("idb", (128, 128), BF16),
        ("ot", (128, 32 * 128), BF16),
        ("es", (128, 1024), F32),
        ("y", (128, D), F32),
        ("sq", (128, D), F32),
        ("stt", (128, 16), F32),
        ("xb", (128, D), BF16),
        ("tstg", (128, 32 * 128), BF16),
        ("ones", (1, 128), F32),
        ("brow", (1, 4096), F32),
        ("brep", (128, 4096), F32),
    ]
    psums = [
        ("pq", (128, 1024), F32),
        ("pt0", (128, 1024), BF16),
        ("pt1", (128, 1024), BF16),
    ]

    def body(em, aps, sb):
        em.dma("gpsimd", sb["idb"][:, :], aps["ident"][:, :], "ldg")
        if has_b:
            em.dma("sync", sb["brow"][:, :], aps["bias"][:, :], "lds")
            em.op("vector", lambda v: v.memset(sb["ones"][:, :], 1.0), "ve")
            em.wait("tensor", "lds")
            em.wait("tensor", "ve")
            emit_bcast(em, sb["ones"], sb["brow"], sb["brep"], 4096, sb["pq"])
        for q in range(4):
            for j in range(32):
                em.wait("gpsimd", "mm")
                em.dma("gpsimd", sb["wq"][:, j * 1024:(j + 1) * 1024],
                       aps["wo"][j * 128:(j + 1) * 128, q * 1024:(q + 1) * 1024], "ldg")
            for t in range(8):
                em.wait("sync", "mm")
                em.dma("sync", sb["ot"][:, :].rearrange("p (j t) -> p j t", t=128),
                       tile3(aps["OT"])[:, :, t * 128:(t + 1) * 128], "lds")
                em.wait("tensor", "lds")
                em.wait("tensor", "ldg")
                em.wait("tensor", "ve")
                for ncol in range(2):
                    for j in range(32):
                        em.op("tensor", lambda tt, ncol=ncol, j=j: tt.matmul(
                            sb["pq"][:, ncol * 512:(ncol + 1) * 512],
                            sb["ot"][:, j * 128:(j + 1) * 128],
                            sb["wq"][:, j * 1024 + ncol * 512: j * 1024 + (ncol + 1) * 512],
                            start=(j == 0), stop=(j == 31)), "mm")
                em.wait("vector", "mm")
                em.wait("vector", "stg")
                em.op("vector", lambda v: v.tensor_copy(out=sb["es"][:, :], in_=sb["pq"][:, :]), "ve")
                em.wait("gpsimd", "ve")
                em.dma("gpsimd", aps["oq"][t * 128:(t + 1) * 128, q * 1024:(q + 1) * 1024], sb["es"][:, :], "stg")
        for t in range(8):
            em.wait("sync", "se")
            em.wait("sync", "stg")
            em.dma("sync", sb["y"][:, :], aps["oq"][t * 128:(t + 1) * 128, :], "lds")
            em.dma("sync", sb["sq"][:, :], aps["xres"][t * 128:(t + 1) * 128, :], "lds")
            em.wait("vector", "lds")
            em.op("vector", lambda v: v.tensor_add(sb["y"][:, :], sb["y"][:, :], sb["sq"][:, :]), "ve")
            if has_b:
                em.op("vector", lambda v: v.tensor_add(sb["y"][:, :], sb["y"][:, :], sb["brep"][:, :]), "ve")

            def fin(t=t):
                em.wait("gpsimd", "ve")
                em.dma("gpsimd", aps["xo"][t * 128:(t + 1) * 128, :], sb["sq"][:, :], "stg")
                em.wait("scalar", "ve")
                em.op("scalar", lambda s: s.activation(out=sb["xb"][:, :], in_=sb["sq"][:, :],
                      func=AF.Copy), "se")
                em.wait("tensor", "se")
                em.wait("vector", "stg")
                emit_transpose_seq(em, [sb["xb"][:, j * 128:(j + 1) * 128] for j in range(32)],
                                   [sb["tstg"][:, j * 128:(j + 1) * 128] for j in range(32)],
                                   sb["pt0"], sb["pt1"], sb["idb"])
                em.wait("gpsimd", "ve")
                for j in range(32):
                    em.dma("gpsimd", aps["xoT"][j * 128:(j + 1) * 128, t * 128:(t + 1) * 128],
                           sb["tstg"][:, j * 128:(j + 1) * 128], "stg")
            emit_ln(em, sb["y"][:, :], D, sb["stt"], sb["sq"][:, :], fin)

    in_maps = []
    for c in range(NC):
        m = {"OT": OT_l[c], "wo": W, "xres": xres_l[c], "ident": ident}
        if has_b:
            m["bias"] = bias.reshape(1, -1)
        in_maps.append(m)
    res = run_launch(body, tensors, sbufs, psums, in_maps)
    return [r["xo"] for r in res], [r["xoT"] for r in res]


def launch_l0_ffn(x1T_l, x1_l, qwT, scl, zp, bias, ident):
    tensors = [
        ("x1T", (4096, RPC), BF16, "ExternalInput"),
        ("x1", (RPC, D), F32, "ExternalInput"),
        ("qwT", (4096, 4096), I32, "ExternalInput"),
        ("scl", (1, 1), F32, "ExternalInput"),
        ("zp", (1, 1), F32, "ExternalInput"),
        ("bias", (1, 4096), F32, "ExternalInput"),
        ("ident", (128, 128), F32, "ExternalInput"),
        ("x2", (RPC, D), F32, "ExternalOutput"),
        ("x2T", (4096, RPC), BF16, "ExternalOutput"),
        ("oq", (RPC, D), F32, "Internal"),
    ]
    sbufs = [
        ("wq", (128, 32 * 1024), BF16),
        ("idb", (128, 128), BF16),
        ("xt", (128, 32 * 128), BF16),
        ("xrow", (128, D), F32),
        ("xrb", (128, D), BF16),
        ("rsall", (128, 8), F32),
        ("es", (128, 1024), F32),
        ("y", (128, D), F32),
        ("sq", (128, D), F32),
        ("stt", (128, 16), F32),
        ("tstg", (128, 32 * 128), BF16),
        ("ones", (1, 128), F32),
        ("srow", (1, 4), F32),
        ("svec", (128, 4), F32),
        ("brow", (1, 4096), F32),
        ("brep", (128, 4096), F32),
        ("bt", (128, 1), F32),
    ]
    psums = [
        ("pq", (128, 1024), F32),
        ("pt0", (128, 1024), BF16),
        ("pt1", (128, 1024), BF16),
    ]

    def body(em, aps, sb):
        em.dma("gpsimd", sb["idb"][:, :], aps["ident"][:, :], "ldg")
        em.dma("sync", sb["srow"][:, 0:1], aps["scl"][:, :], "lds")
        em.dma("sync", sb["srow"][:, 1:2], aps["zp"][:, :], "lds")
        em.dma("sync", sb["brow"][:, :], aps["bias"][:, :], "lds")
        em.op("vector", lambda v: v.memset(sb["ones"][:, :], 1.0), "ve")
        em.wait("tensor", "lds")
        em.wait("tensor", "ve")
        em.op("tensor", lambda t: t.matmul(sb["pq"][:, 0:4], sb["ones"][:, :], sb["srow"][:, 0:4],
                                           start=True, stop=True), "mm")
        em.wait("vector", "mm")
        em.op("vector", lambda v: v.tensor_copy(out=sb["svec"][:, :], in_=sb["pq"][:, 0:4]), "ve")
        em.op("vector", lambda v: v.tensor_scalar(out=sb["svec"][:, 2:3], in0=sb["svec"][:, 1:2],
              scalar1=sb["svec"][:, 0:1], scalar2=None, op0=ALU.mult), "ve")
        em.op("vector", lambda v: v.tensor_scalar(out=sb["svec"][:, 2:3], in0=sb["svec"][:, 2:3],
              scalar1=-1.0, scalar2=None, op0=ALU.mult), "ve")
        emit_bcast(em, sb["ones"], sb["brow"], sb["brep"], 4096, sb["pq"])
        for t in range(8):
            em.wait("sync", "se")
            em.dma("sync", sb["xrow"][:, :], aps["x1"][t * 128:(t + 1) * 128, :], "lds")
            em.wait("scalar", "lds")
            em.op("scalar", lambda s: s.activation(out=sb["xrb"][:, :], in_=sb["xrow"][:, :], func=AF.Copy), "se")
            em.wait("vector", "se")
            em.op("vector", lambda v, t=t: v.reduce_sum(sb["rsall"][:, t:t + 1], sb["xrb"][:, :], axis=AX.X), "ve")
        for q in range(4):
            for j in range(32):
                em.wait("gpsimd", "mm")
                em.dma("gpsimd", sb["wq"][:, j * 1024:(j + 1) * 1024],
                       aps["qwT"][j * 128:(j + 1) * 128, q * 1024:(q + 1) * 1024], "ldg")
            for t in range(8):
                em.wait("sync", "mm")
                em.dma("sync", sb["xt"][:, :].rearrange("p (j t) -> p j t", t=128),
                       tile3(aps["x1T"])[:, :, t * 128:(t + 1) * 128], "lds")
                em.wait("vector", "se")
                em.op("vector", lambda v, t=t: v.tensor_scalar(out=sb["bt"][:, :], in0=sb["rsall"][:, t:t + 1],
                      scalar1=sb["svec"][:, 2:3], scalar2=None, op0=ALU.mult), "ve")
                em.wait("tensor", "lds")
                em.wait("tensor", "ldg")
                em.wait("tensor", "se")
                em.wait("tensor", "ve")
                for ncol in range(2):
                    for j in range(32):
                        em.op("tensor", lambda tt, ncol=ncol, j=j: tt.matmul(
                            sb["pq"][:, ncol * 512:(ncol + 1) * 512],
                            sb["xt"][:, j * 128:(j + 1) * 128],
                            sb["wq"][:, j * 1024 + ncol * 512: j * 1024 + (ncol + 1) * 512],
                            start=(j == 0), stop=(j == 31)), "mm")
                em.wait("scalar", "mm")
                em.wait("scalar", "ve")
                em.wait("scalar", "stg")
                em.op("scalar", lambda s: s.activation(out=sb["es"][:, :], in_=sb["pq"][:, :],
                      func=AF.Copy, scale=sb["svec"][:, 0:1]), "se")
                em.wait("vector", "se")
                em.wait("vector", "stg")
                em.op("vector", lambda v: v.tensor_scalar(out=sb["es"][:, :], in0=sb["es"][:, :],
                      scalar1=sb["bt"][:, :], scalar2=None, op0=ALU.add), "ve")
                em.wait("gpsimd", "ve")
                em.dma("gpsimd", aps["oq"][t * 128:(t + 1) * 128, q * 1024:(q + 1) * 1024], sb["es"][:, :], "stg")
        for t in range(8):
            em.wait("sync", "se")
            em.wait("sync", "stg")
            em.dma("sync", sb["y"][:, :], aps["oq"][t * 128:(t + 1) * 128, :], "lds")
            em.dma("sync", sb["sq"][:, :], aps["x1"][t * 128:(t + 1) * 128, :], "lds")
            em.wait("vector", "lds")
            em.op("vector", lambda v: v.tensor_add(sb["y"][:, :], sb["y"][:, :], sb["sq"][:, :]), "ve")
            em.op("vector", lambda v: v.tensor_add(sb["y"][:, :], sb["y"][:, :], sb["brep"][:, :]), "ve")

            def fin(t=t):
                em.wait("gpsimd", "ve")
                em.dma("gpsimd", aps["x2"][t * 128:(t + 1) * 128, :], sb["sq"][:, :], "stg")
                em.wait("scalar", "ve")
                em.op("scalar", lambda s: s.activation(out=sb["xrb"][:, :], in_=sb["sq"][:, :],
                      func=AF.Copy), "se")
                em.wait("tensor", "se")
                em.wait("vector", "stg")
                emit_transpose_seq(em, [sb["xrb"][:, j * 128:(j + 1) * 128] for j in range(32)],
                                   [sb["tstg"][:, j * 128:(j + 1) * 128] for j in range(32)],
                                   sb["pt0"], sb["pt1"], sb["idb"])
                em.wait("gpsimd", "ve")
                for j in range(32):
                    em.dma("gpsimd", aps["x2T"][j * 128:(j + 1) * 128, t * 128:(t + 1) * 128],
                           sb["tstg"][:, j * 128:(j + 1) * 128], "stg")
            emit_ln(em, sb["y"][:, :], D, sb["stt"], sb["sq"][:, :], fin)

    in_maps = [{"x1T": x1T_l[c], "x1": x1_l[c], "qwT": qwT,
                "scl": scl.reshape(1, 1), "zp": zp.reshape(1, 1),
                "bias": bias.reshape(1, -1), "ident": ident} for c in range(NC)]
    res = run_launch(body, tensors, sbufs, psums, in_maps)
    return [r["x2"] for r in res], [r["x2T"] for r in res]


def launch_l1_qkv(x2T_b, wq_l, wk_l, wv_l, bq_l, bk_l, bv_l, ident):
    SCL = float(128.0 ** -0.5)
    tensors = [
        ("xT", (4096, S), BF16, "ExternalInput"),
        ("wq", (4096, 1024), F32, "ExternalInput"),
        ("wk", (4096, 1024), F32, "ExternalInput"),
        ("wv", (4096, 1024), F32, "ExternalInput"),
        ("bq", (1, 1024), F32, "ExternalInput"),
        ("bk", (1, 1024), F32, "ExternalInput"),
        ("bv", (1, 1024), F32, "ExternalInput"),
        ("ident", (128, 128), F32, "ExternalInput"),
        ("qT8", (1024, S), BF16, "ExternalOutput"),
        ("kT8", (1024, S), BF16, "ExternalOutput"),
        ("v8", (S, 1024), BF16, "ExternalOutput"),
    ]
    sbufs = [
        ("wp", (128, 32 * 1024), BF16),
        ("idb", (128, 128), BF16),
        ("xt", (128, 32 * 128), BF16),
        ("yb", (128, 1024), BF16),
        ("ones", (1, 128), F32),
        ("brow", (1, 1024), F32),
        ("brep", (128, 1024), F32),
        ("tstg", (128, 8 * 128), BF16),
    ]
    psums = [
        ("pq", (128, 1024), F32),
        ("pt0", (128, 1024), BF16),
        ("pt1", (128, 1024), BF16),
    ]

    def body(em, aps, sb):
        em.dma("gpsimd", sb["idb"][:, :], aps["ident"][:, :], "ldg")
        em.op("vector", lambda v: v.memset(sb["ones"][:, :], 1.0), "ve")
        for ph, (wn, bn, on) in enumerate((("wq", "bq", "qT8"), ("wk", "bk", "kT8"), ("wv", "bv", "v8"))):
            for j in range(32):
                em.wait("gpsimd", "mm")
                em.dma("gpsimd", sb["wp"][:, j * 1024:(j + 1) * 1024],
                       aps[wn][j * 128:(j + 1) * 128, :], "ldg")
            em.wait("sync", "mm")
            em.dma("sync", sb["brow"][:, :], aps[bn][:, :], "lds")
            em.wait("tensor", "lds")
            em.wait("tensor", "ve")
            emit_bcast(em, sb["ones"], sb["brow"], sb["brep"], 1024, sb["pq"])
            if ph == 0:
                em.op("vector", lambda v: v.tensor_scalar(out=sb["brep"][:, :], in0=sb["brep"][:, :],
                      scalar1=SCL, scalar2=None, op0=ALU.mult), "ve")
            for t in range(32):
                em.wait("sync", "mm")
                em.dma("sync", sb["xt"][:, :].rearrange("p (j t) -> p j t", t=128),
                       tile3(aps["xT"])[:, :, t * 128:(t + 1) * 128], "lds")
                em.wait("tensor", "lds")
                em.wait("tensor", "ldg")
                em.wait("tensor", "ve")
                for ncol in range(2):
                    for j in range(32):
                        em.op("tensor", lambda tt, ncol=ncol, j=j: tt.matmul(
                            sb["pq"][:, ncol * 512:(ncol + 1) * 512],
                            sb["xt"][:, j * 128:(j + 1) * 128],
                            sb["wp"][:, j * 1024 + ncol * 512: j * 1024 + (ncol + 1) * 512],
                            start=(j == 0), stop=(j == 31)), "mm")
                em.wait("vector", "mm")
                em.wait("vector", "stg")
                em.op("vector", lambda v, ph=ph: v.scalar_tensor_tensor(
                    out=sb["yb"][:, :], in0=sb["pq"][:, :], scalar=(SCL if ph == 0 else 1.0),
                    in1=sb["brep"][:, :], op0=ALU.mult, op1=ALU.add), "ve")
                if ph == 2:
                    em.wait("gpsimd", "ve")
                    em.dma("gpsimd", aps["v8"][t * 128:(t + 1) * 128, :], sb["yb"][:, :], "stg")
                else:
                    em.wait("vector", "stg")
                    emit_transpose_seq(em, [sb["yb"][:, j * 128:(j + 1) * 128] for j in range(8)],
                                       [sb["tstg"][:, j * 128:(j + 1) * 128] for j in range(8)],
                                       sb["pt0"], sb["pt1"], sb["idb"])
                    em.wait("gpsimd", "ve")
                    for j in range(8):
                        em.dma("gpsimd", aps[on][j * 128:(j + 1) * 128, t * 128:(t + 1) * 128],
                               sb["tstg"][:, j * 128:(j + 1) * 128], "stg")

    in_maps = [{"xT": x2T_b[c // 4], "wq": wq_l[c], "wk": wk_l[c], "wv": wv_l[c],
                "bq": bq_l[c], "bk": bk_l[c], "bv": bv_l[c], "ident": ident} for c in range(NC)]
    res = run_launch(body, tensors, sbufs, psums, in_maps)
    return [r["qT8"] for r in res], [r["kT8"] for r in res], [r["v8"] for r in res]


def launch_l1_ffn(x3T_l, x3_l, WT, bias, At, B2, coeffs, scales, ident):
    tensors = [
        ("x3T", (4096, RPC), BF16, "ExternalInput"),
        ("x3", (RPC, D), F32, "ExternalInput"),
        ("wT", (4096, 4096), F32, "ExternalInput"),
        ("bias", (1, 4096), F32, "ExternalInput"),
        ("At", (4096, 256), F32, "ExternalInput"),
        ("B2", (256, 4096), F32, "ExternalInput"),
        ("cf", (1, 16), F32, "ExternalInput"),
        ("sc", (1, 16), F32, "ExternalInput"),
        ("ident", (128, 128), F32, "ExternalInput"),
        ("out", (RPC, D), F32, "ExternalOutput"),
        ("qscr", (RPC, D), F32, "Internal"),
        ("uscr", (RPC, D), F32, "Internal"),
    ]
    sbufs = [
        ("wq", (128, 32 * 1024), BF16),
        ("idb", (128, 128), BF16),
        ("xt", (128, 32 * 128), BF16),
        ("xa", (128, 256), F32),
        ("xab", (128, 256), BF16),
        ("xaT", (128, 2 * 128), BF16),
        ("es", (128, 1024), F32),
        ("upd", (128, 4096), F32),
        ("y", (128, D), F32),
        ("sq", (128, D), F32),
        ("stt", (128, 16), F32),
        ("ones", (1, 128), F32),
        ("scrow", (1, 16), F32),
        ("w16row", (1, 16), F32),
        ("w16r", (128, 16), F32),
        ("brow", (1, 4096), F32),
        ("brep", (128, 4096), F32),
    ]
    psums = [
        ("pq", (128, 1024), F32),
        ("pu", (128, 512), F32),
        ("pt0", (128, 1024), BF16),
        ("pbc", (128, 16), F32),
    ]

    def body(em, aps, sb):
        at = sb["wq"][:, 0:32 * 256]
        b2 = sb["wq"][:, 32 * 256:32 * 256 + 2 * 4096]
        em.dma("gpsimd", sb["idb"][:, :], aps["ident"][:, :], "ldg")
        em.dma("sync", sb["w16row"][:, :], aps["cf"][:, :], "lds")
        em.dma("sync", sb["scrow"][:, :], aps["sc"][:, :], "lds")
        em.dma("sync", sb["brow"][:, :], aps["bias"][:, :], "lds")
        em.op("vector", lambda v: v.memset(sb["ones"][:, :], 1.0), "ve")
        em.wait("vector", "lds")
        em.op("vector", lambda v: v.tensor_mul(sb["w16row"][:, :], sb["w16row"][:, :], sb["scrow"][:, :]), "ve")
        em.wait("tensor", "ve")
        em.op("tensor", lambda t: t.matmul(sb["pbc"][:, :], sb["ones"][:, :], sb["w16row"][:, :],
                                           start=True, stop=True), "mm")
        em.wait("vector", "mm")
        em.op("vector", lambda v: v.tensor_copy(out=sb["w16r"][:, :], in_=sb["pbc"][:, :]), "ve")
        emit_bcast(em, sb["ones"], sb["brow"], sb["brep"], 4096, sb["pq"])
        for j in range(32):
            em.dma("gpsimd", at[:, j * 256:(j + 1) * 256], aps["At"][j * 128:(j + 1) * 128, :], "ldg")
        for j in range(2):
            em.dma("gpsimd", b2[:, j * 4096:(j + 1) * 4096], aps["B2"][j * 128:(j + 1) * 128, :], "ldg")
        em.wait("tensor", "ldg")
        for t in range(8):
            em.wait("sync", "mm")
            em.dma("sync", sb["xt"][:, :].rearrange("p (j t) -> p j t", t=128),
                   tile3(aps["x3T"])[:, :, t * 128:(t + 1) * 128], "lds")
            em.wait("tensor", "lds")
            em.wait("tensor", "ve")
            em.wait("tensor", "se")
            for j in range(32):
                em.op("tensor", lambda tt, j=j: tt.matmul(
                    sb["pq"][:, 0:256], sb["xt"][:, j * 128:(j + 1) * 128],
                    at[:, j * 256:(j + 1) * 256], start=(j == 0), stop=(j == 31)), "mm")
            em.wait("vector", "mm")
            em.wait("vector", "stg")
            em.op("vector", lambda v: v.tensor_copy(out=sb["xa"][:, :], in_=sb["pq"][:, 0:256]), "ve")
            for e in range(16):
                em.op("vector", lambda v, e=e: v.tensor_scalar(out=sb["xa"][:, e * 16:(e + 1) * 16],
                      in0=sb["xa"][:, e * 16:(e + 1) * 16], scalar1=sb["w16r"][:, e:e + 1],
                      scalar2=None, op0=ALU.mult), "ve")
            em.wait("scalar", "ve")
            em.op("scalar", lambda s: s.activation(out=sb["xab"][:, :], in_=sb["xa"][:, :], func=AF.Copy), "se")
            em.wait("tensor", "se")
            em.wait("tensor", "ve")
            for j in range(2):
                em.op("tensor", lambda tt, j=j: tt.transpose(sb["pt0"][:, j * 128:(j + 1) * 128],
                      sb["xab"][:, j * 128:(j + 1) * 128], sb["idb"][:, :]), "mm")
            em.wait("vector", "mm")
            em.op("vector", lambda v: v.tensor_copy(out=sb["xaT"][:, :], in_=sb["pt0"][:, 0:256]), "ve")
            em.wait("tensor", "ve")
            for ncol in range(8):
                for j in range(2):
                    em.op("tensor", lambda tt, ncol=ncol, j=j: tt.matmul(
                        sb["pu"][:, :], sb["xaT"][:, j * 128:(j + 1) * 128],
                        b2[:, j * 4096 + ncol * 512: j * 4096 + (ncol + 1) * 512],
                        start=(j == 0), stop=(j == 1)), "mm")
                em.wait("vector", "mm")
                em.op("vector", lambda v, ncol=ncol: v.tensor_copy(out=sb["upd"][:, ncol * 512:(ncol + 1) * 512],
                      in_=sb["pu"][:, :]), "ve")
                em.wait("tensor", "ve")
            em.wait("gpsimd", "ve")
            em.wait("gpsimd", "stg")
            em.dma("gpsimd", aps["uscr"][t * 128:(t + 1) * 128, :], sb["upd"][:, :], "stg")
        for q in range(4):
            for j in range(32):
                em.wait("gpsimd", "mm")
                em.dma("gpsimd", sb["wq"][:, j * 1024:(j + 1) * 1024],
                       aps["wT"][j * 128:(j + 1) * 128, q * 1024:(q + 1) * 1024], "ldg")
            for t in range(8):
                em.wait("sync", "mm")
                em.dma("sync", sb["xt"][:, :].rearrange("p (j t) -> p j t", t=128),
                       tile3(aps["x3T"])[:, :, t * 128:(t + 1) * 128], "lds")
                em.wait("tensor", "lds")
                em.wait("tensor", "ldg")
                em.wait("tensor", "ve")
                for ncol in range(2):
                    for j in range(32):
                        em.op("tensor", lambda tt, ncol=ncol, j=j: tt.matmul(
                            sb["pq"][:, ncol * 512:(ncol + 1) * 512],
                            sb["xt"][:, j * 128:(j + 1) * 128],
                            sb["wq"][:, j * 1024 + ncol * 512: j * 1024 + (ncol + 1) * 512],
                            start=(j == 0), stop=(j == 31)), "mm")
                em.wait("vector", "mm")
                em.wait("vector", "stg")
                em.op("vector", lambda v: v.tensor_copy(out=sb["es"][:, :], in_=sb["pq"][:, :]), "ve")
                em.wait("gpsimd", "ve")
                em.dma("gpsimd", aps["qscr"][t * 128:(t + 1) * 128, q * 1024:(q + 1) * 1024], sb["es"][:, :], "stg")
        for t in range(8):
            em.wait("sync", "se")
            em.wait("sync", "stg")
            em.dma("sync", sb["y"][:, :], aps["qscr"][t * 128:(t + 1) * 128, :], "lds")
            em.dma("sync", sb["sq"][:, :], aps["uscr"][t * 128:(t + 1) * 128, :], "lds")
            em.wait("vector", "lds")
            em.op("vector", lambda v: v.tensor_add(sb["y"][:, :], sb["y"][:, :], sb["sq"][:, :]), "ve")
            em.wait("sync", "ve")
            em.dma("sync", sb["sq"][:, :], aps["x3"][t * 128:(t + 1) * 128, :], "lds")
            em.wait("vector", "lds")
            em.op("vector", lambda v: v.tensor_add(sb["y"][:, :], sb["y"][:, :], sb["sq"][:, :]), "ve")
            em.op("vector", lambda v: v.tensor_add(sb["y"][:, :], sb["y"][:, :], sb["brep"][:, :]), "ve")

            def fin(t=t):
                em.wait("gpsimd", "ve")
                em.dma("gpsimd", aps["out"][t * 128:(t + 1) * 128, :], sb["sq"][:, :], "stg")
            emit_ln(em, sb["y"][:, :], D, sb["stt"], sb["sq"][:, :], fin)

    in_maps = [{"x3T": x3T_l[c], "x3": x3_l[c], "wT": WT, "bias": bias.reshape(1, -1),
                "At": At, "B2": B2, "cf": coeffs.reshape(1, -1), "sc": scales.reshape(1, -1),
                "ident": ident} for c in range(NC)]
    res = run_launch(body, tensors, sbufs, psums, in_maps)
    return [r["out"] for r in res]



def _rope_tables():
    pos = np.arange(S, dtype=np.float32)
    inv = (1.0 / (10000.0 ** (np.arange(0, 64, 2, dtype=np.float32) / np.float32(64.0)))).astype(np.float32)
    fr = np.outer(pos, inv).astype(np.float32)
    emb = np.concatenate([fr, fr], axis=1)
    return np.cos(emb).astype(np.float32), np.sin(emb).astype(np.float32)


def _alibi_slopes():
    import math
    n = 32
    base = 2.0 ** (-(2.0 ** -(math.log2(n) - 3)))
    return np.power(base, np.arange(1, n + 1, dtype=np.float32)).astype(np.float32)


def kernel(L0_attn_hidden_states, L0_attn_Wq_a, L0_attn_q_a_ln_w, L0_attn_Wq_b,
           L0_attn_Wkv_a, L0_attn_kv_a_ln_w, L0_attn_Wkv_b, L0_attn_Wo,
           L0_ffn_qweight, L0_ffn_scale, L0_ffn_zero_point, L0_ffn_bias,
           L1_attn_w_qkv, L1_attn_b_qkv, L1_attn_w_out, L1_attn_b_out,
           L1_ffn_W, L1_ffn_bias, L1_ffn_A, L1_ffn_Bf, L1_ffn_coeffs, L1_ffn_scales):
    X = np.ascontiguousarray(np.asarray(L0_attn_hidden_states, np.float32).reshape(T, D))
    ident = np.eye(128, dtype=np.float32)
    tri = np.ascontiguousarray(np.triu(np.full((128, 128), -1e9, np.float32), 1))
    cos0, sin0 = _rope_tables()

    perm_pe = np.concatenate([np.arange(0, 64, 2), np.arange(1, 64, 2)])
    Wkva_p = np.ascontiguousarray(np.concatenate(
        [np.asarray(L0_attn_Wkv_a, np.float32)[:, :512],
         np.asarray(L0_attn_Wkv_a, np.float32)[:, 512 + perm_pe]], axis=1))
    QD = 192
    nope_cols = np.concatenate([np.arange(h * QD, h * QD + 128) for h in range(32)])
    pe_cols = np.concatenate([h * QD + 128 + perm_pe for h in range(32)])
    Wqb_p = np.ascontiguousarray(np.asarray(L0_attn_Wq_b, np.float32)[:, np.concatenate([nope_cols, pe_cols])])
    wkvb = np.asarray(L0_attn_Wkv_b, np.float32)
    kcols = np.concatenate([np.arange(h * 256, h * 256 + 128) for h in range(32)])
    vcols = np.concatenate([np.arange(h * 256 + 128, h * 256 + 256) for h in range(32)])
    Wkvb_p = np.ascontiguousarray(wkvb[:, np.concatenate([kcols, vcols])])

    qanT_l, cnT_l, kpeT_l = launch_l0_down(X, np.ascontiguousarray(np.asarray(L0_attn_Wq_a, np.float32)),
                                           np.asarray(L0_attn_q_a_ln_w, np.float32),
                                           Wkva_p, np.asarray(L0_attn_kv_a_ln_w, np.float32),
                                           cos0, sin0, ident)
    qT_l = launch_l0_qb(qanT_l, Wqb_p, cos0, sin0, ident)
    kT_l, v_l = launch_l0_kvup(cnT_l, Wkvb_p)

    def cat_cols(lst, b):
        return np.ascontiguousarray(np.concatenate([np.asarray(lst[b * 4 + i]) for i in range(4)], axis=1))

    def cat_rows(lst, b):
        return np.ascontiguousarray(np.concatenate([np.asarray(lst[b * 4 + i]) for i in range(4)], axis=0))

    qT_b = [cat_cols(qT_l, b) for b in range(2)]
    kT_b = [cat_cols(kT_l, b) for b in range(2)]
    v_b = [cat_rows(v_l, b) for b in range(2)]
    kpeT_b = [cat_cols(kpeT_l, b) for b in range(2)]

    qTn_in, qTp_in, kT_in, v_in, kpe_in = [], [], [], [], []
    for c in range(NC):
        b, g = c // 4, c % 4
        qTn_in.append(np.ascontiguousarray(qT_b[b][g * 1024:(g + 1) * 1024]))
        qTp_in.append(np.ascontiguousarray(qT_b[b][4096 + g * 512: 4096 + (g + 1) * 512]))
        kT_in.append(np.ascontiguousarray(kT_b[b][g * 1024:(g + 1) * 1024]))
        v_in.append(np.ascontiguousarray(v_b[b][:, g * 1024:(g + 1) * 1024]))
        kpe_in.append(kpeT_b[b])
    OT8_l = launch_attn("l0", qTn_in, kT_in, v_in, qTp_l=qTp_in, kpeT_l=kpe_in, tri=tri)

    OT_b = [cat_rows(OT8_l, b) for b in range(2)]
    OT_in = [np.ascontiguousarray(OT_b[c // 4][:, (c % 4) * 1024:((c % 4) + 1) * 1024]) for c in range(NC)]
    xres_l = [np.ascontiguousarray(X[c * RPC:(c + 1) * RPC]) for c in range(NC)]
    x1_l, x1T_l = launch_oproj(OT_in, np.ascontiguousarray(np.asarray(L0_attn_Wo, np.float32)), None, xres_l, ident)

    qwT = np.ascontiguousarray(np.asarray(L0_ffn_qweight, np.int32).T)
    x2_l, x2T_l = launch_l0_ffn(x1T_l, x1_l, qwT, np.asarray(L0_ffn_scale, np.float32),
                                np.asarray(L0_ffn_zero_point, np.float32),
                                np.asarray(L0_ffn_bias, np.float32), ident)

    x2T_b = [cat_cols(x2T_l, b) for b in range(2)]
    wqkv = np.asarray(L1_attn_w_qkv, np.float32)
    bqkv = np.asarray(L1_attn_b_qkv, np.float32)
    wq_l, wk_l, wv_l, bq_l, bk_l, bv_l = [], [], [], [], [], []
    for c in range(NC):
        g = c % 4
        wq_l.append(np.ascontiguousarray(wqkv[:, g * 1024:(g + 1) * 1024]))
        wk_l.append(np.ascontiguousarray(wqkv[:, 4096 + g * 1024:4096 + (g + 1) * 1024]))
        wv_l.append(np.ascontiguousarray(wqkv[:, 8192 + g * 1024:8192 + (g + 1) * 1024]))
        bq_l.append(np.ascontiguousarray(bqkv[g * 1024:(g + 1) * 1024].reshape(1, -1)))
        bk_l.append(np.ascontiguousarray(bqkv[4096 + g * 1024:4096 + (g + 1) * 1024].reshape(1, -1)))
        bv_l.append(np.ascontiguousarray(bqkv[8192 + g * 1024:8192 + (g + 1) * 1024].reshape(1, -1)))
    qT8_l, kT8_l, v8_l = launch_l1_qkv(x2T_b, wq_l, wk_l, wv_l, bq_l, bk_l, bv_l, ident)

    slopes = _alibi_slopes()
    slopes_l = [np.ascontiguousarray(slopes[(c % 4) * 8:((c % 4) + 1) * 8].reshape(1, 8)) for c in range(NC)]
    OT8b_l = launch_attn("l1", qT8_l, kT8_l, v8_l, tri=tri, slopes_l=slopes_l)

    OTb_b = [cat_rows(OT8b_l, b) for b in range(2)]
    OTb_in = [np.ascontiguousarray(OTb_b[c // 4][:, (c % 4) * 1024:((c % 4) + 1) * 1024]) for c in range(NC)]
    x3_l, x3T_l = launch_oproj(OTb_in, np.ascontiguousarray(np.asarray(L1_attn_w_out, np.float32)),
                               np.asarray(L1_attn_b_out, np.float32), x2_l, ident)

    WT = np.ascontiguousarray(np.asarray(L1_ffn_W, np.float32).T)
    At = np.ascontiguousarray(np.asarray(L1_ffn_A, np.float32).transpose(2, 0, 1).reshape(4096, 256))
    B2 = np.ascontiguousarray(np.asarray(L1_ffn_Bf, np.float32).transpose(0, 2, 1).reshape(256, 4096))
    out_l = launch_l1_ffn(x3T_l, x3_l, WT, np.asarray(L1_ffn_bias, np.float32),
                          At, B2, np.asarray(L1_ffn_coeffs, np.float32),
                          np.asarray(L1_ffn_scales, np.float32), ident)

    out = np.concatenate([np.asarray(o, np.float32) for o in out_l], axis=0).reshape(2, S, D)
    return out
